# version 27
# baseline (speedup 1.0000x reference)
import numpy as np
import ml_dtypes

import concourse.bass as bass
import concourse.mybir as mybir
from concourse.alu_op_type import AluOpType as OP
from concourse.bass_utils import run_bass_kernel_spmd

F32 = mybir.dt.float32
BF16 = mybir.dt.bfloat16
AF = mybir.ActivationFunctionType
AX = mybir.AxisListType

D = 1024
DC = 8
H = 16
HD = 64
E = 8
F = 4096
NQ = 4
FQ = F // NQ
FT_Q = FQ // 128
LANE = 128
WA = HD + 1
W_AUG = H * WA


class _MockInst:
    def then_inc(self, *a, **k):
        return self


class _MockEng:
    def wait_ge(self, *a, **k):
        pass

    def __getattr__(self, name):
        return lambda *a, **k: _MockInst()


def build_program(T_full, T_own, pb_cap=1024):
    TF_T = T_full // LANE
    TO_T = T_own // LANE
    KCH = T_full // 512
    QCH = T_own // 512
    PB = min(pb_cap, T_own)
    NPAIR = T_own // PB
    NHALF = PB // 512
    JT = PB // 128
    NPH = E * NQ
    CAP = T_own // 4
    CT = CAP // 128
    NSLOT = E * CAP

    nc = bass.Bass("TRN2", target_bir_lowering=False)

    xf = nc.dram_tensor("xf", [T_full, D], F32, kind="ExternalInput").ap()
    rotT_h = nc.dram_tensor("rotT", [HD, H], F32, kind="ExternalInput").ap()
    entT_h = nc.dram_tensor("entT", [HD, H], F32, kind="ExternalInput").ap()
    wr_h = nc.dram_tensor("wr", [LANE, DC * E], F32, kind="ExternalInput").ap()
    br_h = nc.dram_tensor("br", [E, 1], F32, kind="ExternalInput").ap()
    b1_h = nc.dram_tensor("b1", [LANE, E * 32], F32, kind="ExternalInput").ap()
    b2_h = nc.dram_tensor("b2", [E, D], F32, kind="ExternalInput").ap()
    w1_h = nc.dram_tensor("w1", [E, NQ, LANE, DC * FQ], F32, kind="ExternalInput").ap()
    w2_h = nc.dram_tensor("w2", [E, NQ, LANE, FT_Q * D], F32, kind="ExternalInput").ap()
    xg_h = nc.dram_tensor("xg", [NSLOT, D], BF16, kind="ExternalInput").ap()
    bases_h = nc.dram_tensor("bases", [E, 1], F32, kind="ExternalInput").ap()
    yscr_h = nc.dram_tensor("yscr", [NSLOT, D], BF16, kind="Internal").ap()
    w1b_h = nc.dram_tensor("w1b", [E, NQ, LANE, DC * FQ], BF16, kind="Internal").ap()
    w2b_h = nc.dram_tensor("w2b", [E, NQ, LANE, FT_Q * D], BF16, kind="Internal").ap()
    out_h = nc.dram_tensor("out", [T_own, D], F32, kind="ExternalOutput").ap()

    def sb(name, shape, dt):
        cm = nc.sbuf_tensor(name, shape, dt)
        return cm.__enter__(), cm

    def pst(name, shape, dt):
        cm = nc.psum_tensor(name, shape, dt)
        return cm.__enter__(), cm

    acc, _ = sb("acc", [LANE, TO_T * D], F32)
    consts, _ = sb("consts", [LANE, 6], F32)
    cT, _ = sb("cT", [HD, H], F32)
    rot_s, _ = sb("rot_s", [HD, H], F32)
    ent_s, _ = sb("ent_s", [HD, H], F32)
    wr_s, _ = sb("wr_s", [LANE, DC * E], F32)
    wr_hi, _ = sb("wr_hi", [LANE, DC * E], BF16)
    wr_lo, _ = sb("wr_lo", [LANE, DC * E], BF16)
    br_s, _ = sb("br_s", [E, 1], F32)
    b1_s, _ = sb("b1_s", [LANE, E * 32], F32)
    ident, _ = sb("ident", [LANE, LANE], BF16)
    identf, _ = sb("identf", [LANE, LANE], F32)
    ones64, _ = sb("ones64", [HD, 1], BF16)
    gates, _ = sb("gates", [LANE, TO_T * E], F32)
    lw_recip, _ = sb("lw_recip", [LANE, 1], F32)
    bases_s, _ = sb("bases_s", [E, 1], F32)
    pos_i, _ = sb("pos_i", [LANE, TO_T], mybir.dt.int32)

    xnt, xnt_cm = sb("xnt", [LANE, TF_T * W_AUG], BF16)

    logitsT, k1 = sb("logitsT", [E, T_own], F32)
    lwork, k2 = sb("lwork", [LANE, 24], F32)
    gateT, k3 = sb("gateT", [E, T_own], BF16)
    b2_s, k4 = sb("b2_s", [E, D], BF16)
    p2keep_cms = [k1, k2, k3, k4]

    lntmp, c1 = sb("lntmp", [LANE, D], F32)
    hitmp, c2 = sb("hitmp", [LANE, D], BF16)
    lotmp, c3 = sb("lotmp", [LANE, D], BF16)
    sqscr, c4 = sb("sqscr", [LANE, D], BF16)
    stat, c5 = sb("stat", [LANE, 16], F32)
    xst0, c6 = sb("xstage0", [LANE, D], F32)
    xst1, c7 = sb("xstage1", [LANE, D], F32)
    xst = [xst0, xst1]
    hiT, c8 = sb("hiT", [LANE, D], BF16)
    loT, c9 = sb("loT", [LANE, D], BF16)
    p12_cms = [c1, c2, c3, c4, c5, c6, c7, c8, c9]

    ps_tr0, d1 = pst("ps_tr0", [LANE, LANE], BF16)
    ps_tr1, d2 = pst("ps_tr1", [LANE, LANE], BF16)
    ps_tr2, d1b = pst("ps_tr2", [LANE, LANE], BF16)
    ps_tr3, d2b = pst("ps_tr3", [LANE, LANE], BF16)
    ps_tr = [ps_tr0, ps_tr1, ps_tr2, ps_tr3]
    ps_lg0, d3 = pst("ps_lg0", [E, LANE], F32)
    ps_lg1, d4 = pst("ps_lg1", [E, LANE], F32)
    ps_lg = [ps_lg0, ps_lg1]
    p12_pcms = [d1, d2, d1b, d2b, d3, d4]

    SEMS = {}
    for n in ["ldx", "ldx0", "ldx1", "qmh", "ldw0", "ldw1", "ldb", "g", "t", "s", "v", "o",
              "ldw2", "sct0", "sct1", "xg0", "xg1", "ys", "gt0", "gt1", "wc"]:
        cm = nc.semaphore(n)
        SEMS[n] = cm.__enter__()

    chunks = []
    M = {}
    C = {}
    strict = [False]

    def mk(engname):
        def adder(fn):
            chunks.append((engname, fn))
        return adder

    def inc(inst, semname, label=None, k=1):
        C[semname] = C.get(semname, 0) + k
        if label is not None:
            if label in M:
                assert M[label] == (semname, C[semname]), (
                    label, M[label], (semname, C[semname]))
            M[label] = (semname, C[semname])
        if not isinstance(inst, _MockInst):
            inst.then_inc(SEMS[semname], k)

    def w(e, label):
        if label not in M:
            assert not strict[0], f"forward ref: {label}"
            return
        s, vv = M[label]
        e.wait_ge(SEMS[s], vv)

    def xnt_tile3(t):
        return xnt[:, t * W_AUG:(t + 1) * W_AUG].rearrange("p (hh ww) -> p hh ww", ww=WA)

    def xnt_heads(t):
        return xnt_tile3(t)[:, :, 0:HD]

    def xnt_head(t, h):
        return xnt[:, t * W_AUG + h * WA: t * W_AUG + h * WA + HD]

    def xnt_vaug(t, h):
        return xnt[:, t * W_AUG + h * WA: t * W_AUG + (h + 1) * WA]

    def hitmp3():
        return hitmp[:].rearrange("p (hh ww) -> p hh ww", ww=HD)

    @mk("sync")
    def _(e):
        inc(e.dma_start(out=acc[:].rearrange("p (t d) -> p t d", d=D),
                        in_=xf[0:T_own, :].rearrange("(t p) d -> p t d", p=LANE)),
            "ldx", None, 16)
        inc(e.dma_start(out=rot_s[:], in_=rotT_h[:]), "ldx", None, 16)
        inc(e.dma_start(out=ent_s[:], in_=entT_h[:]), "ldx", None, 16)
        inc(e.dma_start(out=wr_s[:], in_=wr_h[:]), "ldx", None, 16)
        inc(e.dma_start(out=br_s[:], in_=br_h[:]), "ldx", None, 16)
        inc(e.dma_start(out=br_s[:], in_=br_h[:]), "ldx", None, 16) if False else None
        inc(e.dma_start(out=bases_s[:], in_=bases_h[:]), "ldx", None, 16)
        inc(e.dma_start(out=b1_s[:], in_=b1_h[:]), "ldx", "ld_p0", 16)

    @mk("gpsimd")
    def _(e):
        inc(e.dma_start(out=b2_s[:], in_=b2_h[:]), "ldb", "ld_b2", 16)
        e.memset(ident[:], 0.0)
        e.memset(identf[:], 0.0)
        e.drain()
        e.affine_select(out=ident[:], in_=ident[:], compare_op=OP.not_equal,
                        fill=1.0, base=0, pattern=[[-1, LANE]], channel_multiplier=1)
        e.affine_select(out=identf[:], in_=identf[:], compare_op=OP.not_equal,
                        fill=1.0, base=0, pattern=[[-1, LANE]], channel_multiplier=1)
        e.memset(ones64[:], 1.0)
        last = None
        for t in range(TF_T):
            last = e.memset(xnt_tile3(t)[:, :, HD:WA], 1.0)
        inc(last, "g", "g_const")

    @mk("vector")
    def _(e):
        e.memset(consts[:, 0:1], 1.0 / D)
        e.memset(consts[:, 1:2], 1e-5)
        e.memset(consts[:, 2:3], -1.0)
        e.memset(consts[:, 3:4], 0.0)
        e.memset(consts[:, 4:5], 1.0)
        w(e, "ld_p0")
        e.tensor_tensor(cT[:], rot_s[:], ent_s[:], OP.mult)
        e.drain()
        e.tensor_scalar(cT[:], cT[:], 0.125, None, OP.mult)
        e.tensor_copy(wr_hi[:], wr_s[:])
        e.drain()
        inc(e.tensor_tensor(wr_lo[:], wr_s[:], wr_hi[:], OP.subtract), "v", "v_const")

    def p1_A(t):
        own = t < TO_T
        i = t - TO_T
        so = (t % 2) * 8

        if not own:
            @mk("sync")
            def _(e, t=t, i=i):
                if i >= 2:
                    w(e, f"xnt_{t - 2}")
                inc(e.dma_start(out=xst[i % 2][:], in_=xf[t * LANE:(t + 1) * LANE, :]),
                    f"ldx{i % 2}", f"xoth_{t}", 16)

        @mk("vector")
        def _(e, t=t, own=own, i=i, so=so):
            if own:
                if t == 0:
                    w(e, "ld_p0")
                srcap = acc[:, t * D:(t + 1) * D]
            else:
                w(e, f"xoth_{t}")
                srcap = xst[i % 2][:]
            inc(e.tensor_reduce(stat[:, so:so + 1], srcap, AX.X, OP.add),
                "v", f"sum_{t}")

        @mk("scalar")
        def _(e, t=t, own=own, i=i, so=so):
            w(e, f"sum_{t}")
            srcap = acc[:, t * D:(t + 1) * D] if own else xst[i % 2][:]
            e.drain()
            inc(e.activation(sqscr[:], srcap, AF.Square,
                             accum_out=stat[:, so + 1:so + 2]), "s", f"sqs_{t}")

    def p1_B(t):
        own = t < TO_T
        i = t - TO_T
        so = (t % 2) * 8

        @mk("vector")
        def _(e, t=t, so=so):
            w(e, f"sqs_{t}")
            e.tensor_scalar(stat[:, so + 2:so + 3], stat[:, so:so + 1],
                            consts[:, 0:1], None, OP.mult)
            e.tensor_scalar(stat[:, so + 3:so + 4], stat[:, so + 1:so + 2],
                            consts[:, 0:1], None, OP.mult)
            e.drain()
            e.tensor_tensor(stat[:, so + 4:so + 5], stat[:, so + 2:so + 3],
                            stat[:, so + 2:so + 3], OP.mult)
            e.drain()
            inc(e.tensor_tensor(stat[:, so + 4:so + 5], stat[:, so + 3:so + 4],
                                stat[:, so + 4:so + 5], OP.subtract), "v", f"var_{t}")

        @mk("scalar")
        def _(e, t=t, so=so):
            w(e, f"var_{t}")
            inc(e.activation(stat[:, so + 5:so + 6], stat[:, so + 4:so + 5], AF.Sqrt,
                             bias=consts[:, 1:2], scale=1.0), "s", f"sd_{t}")

        if own:
            @mk("vector")
            def _(e, t=t, so=so):
                w(e, f"sd_{t}")
                e.reciprocal(stat[:, so + 6:so + 7], stat[:, so + 5:so + 6])
                e.drain()
                e.tensor_scalar(lntmp[:], acc[:, t * D:(t + 1) * D],
                                stat[:, so + 2:so + 3], stat[:, so + 6:so + 7],
                                OP.subtract, OP.mult)
                e.drain()
                if t > 0:
                    w(e, f"trdone_{t - 1}")
                e.tensor_copy(hitmp[:], lntmp[:])
                e.drain()
                w(e, "g_const")
                inc(e.tensor_copy(xnt_heads(t), hitmp3()), "v", f"xnt_{t}")
                inc(e.tensor_tensor(lotmp[:], lntmp[:], hitmp[:], OP.subtract),
                    "v", f"ln_{t}")

            @mk("tensor")
            def _(e, t=t):
                w(e, f"ln_{t}")
                for c in range(DC):
                    hslot = (c % 2) * 2
                    lslot = (c % 2) * 2 + 1
                    if c > 1:
                        w(e, f"evh_{t}_{c - 2}")
                    elif t > 0:
                        w(e, f"evh_{t - 1}_{DC - 2 + c}")
                    inc(e.transpose(ps_tr[hslot][:], hitmp[:, c * 128:(c + 1) * 128],
                                    ident[:]), "t", f"trh_{t}_{c}")
                    if c > 1:
                        w(e, f"evl_{t}_{c - 2}")
                    elif t > 0:
                        w(e, f"evl_{t - 1}_{DC - 2 + c}")
                    inc(e.transpose(ps_tr[lslot][:], lotmp[:, c * 128:(c + 1) * 128],
                                    ident[:]), "t", f"trl_{t}_{c}")
                    if c == DC - 1:
                        M[f"trdone_{t}"] = M[f"trl_{t}_{c}"]

            @mk("vector")
            def _(e, t=t):
                for c in range(DC):
                    hslot = (c % 2) * 2
                    lslot = (c % 2) * 2 + 1
                    w(e, f"trh_{t}_{c}")
                    if t > 0 and c == 0:
                        w(e, f"lgt_{t - 1}")
                    inc(e.tensor_copy(hiT[:, c * 128:(c + 1) * 128], ps_tr[hslot][:]),
                        "v", f"evh_{t}_{c}")
                    w(e, f"trl_{t}_{c}")
                    inc(e.tensor_copy(loT[:, c * 128:(c + 1) * 128], ps_tr[lslot][:]),
                        "v", f"evl_{t}_{c}")

            @mk("tensor")
            def _(e, t=t):
                par = t % 2
                if t >= 2:
                    w(e, f"lge_{t - 2}")
                w(e, "v_const")
                i1 = None
                for c in range(DC):
                    w(e, f"evh_{t}_{c}")
                    w(e, f"evl_{t}_{c}")
                    hT = hiT[:, c * 128:(c + 1) * 128]
                    lT = loT[:, c * 128:(c + 1) * 128]
                    i1 = e.matmul(ps_lg[par][:], wr_hi[:, c * E:(c + 1) * E], hT,
                                  start=(c == 0), stop=False)
                    i1 = e.matmul(ps_lg[par][:], wr_lo[:, c * E:(c + 1) * E], hT,
                                  start=False, stop=False)
                    i1 = e.matmul(ps_lg[par][:], wr_hi[:, c * E:(c + 1) * E], lT,
                                  start=False, stop=(c == DC - 1))
                inc(i1, "t", f"lgt_{t}")

            @mk("vector")
            def _(e, t=t):
                w(e, f"lgt_{t}")
                inc(e.tensor_scalar(logitsT[:, t * 128:(t + 1) * 128], ps_lg[t % 2][:],
                                    br_s[:], None, OP.add), "v", f"lge_{t}")
        else:
            @mk("vector")
            def _(e, t=t, i=i, so=so):
                w(e, f"sd_{t}")
                e.reciprocal(stat[:, so + 6:so + 7], stat[:, so + 5:so + 6])
                e.drain()
                w(e, "g_const")
                inc(e.tensor_scalar(xnt_heads(t), xst[i % 2][:],
                                    stat[:, so + 2:so + 3], stat[:, so + 6:so + 7],
                                    OP.subtract, OP.mult), "v", f"xnt_{t}")

    p1_A(0)
    if TF_T > 1:
        p1_A(1)
    for t in range(TF_T):
        p1_B(t)
        if t + 2 < TF_T:
            p1_A(t + 2)

    IOA = bass.IndirectOffsetOnAxis

    for cm in reversed(p12_cms):
        cm.__exit__(None, None, None)
    for cm in reversed(p12_pcms):
        cm.__exit__(None, None, None)

    kb0, e1 = sb("kb0", [WA, T_full], BF16)
    kb1, e2 = sb("kb1", [WA, T_full], BF16)
    kb2, e1b = sb("kb2", [WA, T_full], BF16)
    kbl = [kb0, kb1, kb2]
    qb0, e3 = sb("qb0", [WA, T_own], BF16)
    qb1, e4 = sb("qb1", [WA, T_own], BF16)
    qb2, e3b = sb("qb2", [WA, T_own], BF16)
    qbl = [qb0, qb1, qb2]
    sq0, e5 = sb("sq0", [HD, 512], BF16)
    sq1, e6 = sb("sq1", [HD, 512], BF16)
    sql = [sq0, sq1]
    pb0, e7 = sb("pb0", [LANE, PB], BF16)
    pb2, e7b = sb("pb2", [LANE, PB], BF16)
    pb3, e7c = sb("pb3", [LANE, PB], BF16)
    pb1, e8 = sb("pb1", [LANE, PB], BF16)
    pbl = [pb0, pb1, pb2, pb3]
    ob0, e9 = sb("ob0", [WA, PB], BF16)
    ob1, e10 = sb("ob1", [WA, PB], BF16)
    obl = [ob0, ob1]
    arow, e11 = sb("arow", [1, T_own], BF16)
    asml, e12 = sb("asml", [1, 4], F32)
    xdt0, e15 = sb("xdt0", [LANE, D], BF16)
    xdt1, e16 = sb("xdt1", [LANE, D], BF16)
    xdt = [xdt0, xdt1]
    p3_cms = [e1, e2, e1b, e3, e4, e3b, e5, e6, e7, e7b, e7c, e8, e9, e10, e11, e12, e15, e16]
    if PB >= D:
        hit2 = [pb0[:, 0:D], pb1[:, 0:D]]
    else:
        h2a, e13 = sb("hit2_0", [LANE, D], BF16)
        h2b, e14 = sb("hit2_1", [LANE, D], BF16)
        hit2 = [h2a[:], h2b[:]]
        p3_cms += [e13, e14]

    psS0, f1 = pst("psS0", [LANE, PB], F32)
    psS1, f2 = pst("psS1", [LANE, PB], F32)
    psS = [psS0, psS1]
    psO0, f3 = pst("psO0", [WA, 512], F32)
    psO1, f4 = pst("psO1", [WA, 512], F32)
    psO = [psO0, psO1]
    psX0, f5 = pst("psX0", [LANE, 1024], BF16)
    psX1, f6 = pst("psX1", [LANE, 1024], BF16)
    psX = [psX0, psX1]
    psX2 = psX
    p3_pcms = [f1, f2, f3, f4, f5, f6]

    ringX = []
    ringS = []

    def ring_next(rg, consumer_label):
        i = len(rg)
        waitlbl = rg[i - 2] if i >= 2 else f"lge_{TO_T - 1}"
        rg.append(consumer_label)
        return i % 2, waitlbl

    def mk_tileA(t):
        spar, swl = ring_next(ringS, f"lmax_{t}")

        @mk("tensor")
        def _(e, t=t, spar=spar, swl=swl):
            if swl:
                w(e, swl)
            w(e, f"lge_{t}")
            inc(e.transpose(psS[spar][0:LANE, 0:E], logitsT[:, t * 128:(t + 1) * 128],
                            identf[0:E, 0:E]), "t", f"lgtr_{t}")

        @mk("vector")
        def _(e, t=t, spar=spar):
            w(e, f"lgtr_{t}")
            if t >= 1:
                w(e, f"gttr_{t - 1}")
            e.tensor_copy(lwork[:, 0:E], psS[spar][0:LANE, 0:E])
            e.drain()
            inc(e.tensor_reduce(lwork[:, 8:9], lwork[:, 0:E], AX.X, OP.max,
                                negate=True), "v", f"lmax_{t}")

        @mk("scalar")
        def _(e, t=t):
            w(e, f"lmax_{t}")
            inc(e.activation(lwork[:, 16:24], lwork[:, 0:E], AF.Exp,
                             bias=lwork[:, 8:9], scale=1.0,
                             accum_out=lwork[:, 9:10]), "s", f"gexp_{t}")

        @mk("vector")
        def _(e, t=t):
            w(e, f"gexp_{t}")
            e.reciprocal(lwork[:, 10:11], lwork[:, 9:10])
            e.tensor_scalar(lwork[:, 11:12], lwork[:, 8:9], consts[:, 2:3], None,
                            OP.mult)
            e.drain()
            bc = bass.broadcast_tensor_aps(lwork[:, 11:12], lwork[:, 0:E])[0]
            e.tensor_tensor(gates[:, t * E:(t + 1) * E], lwork[:, 0:E], bc,
                            OP.is_equal)
            e.drain()
            inc(e.tensor_scalar(gates[:, t * E:(t + 1) * E],
                                gates[:, t * E:(t + 1) * E], lwork[:, 10:11],
                                None, OP.mult), "v", f"gts_{t}")
            e.drain()
            inc(e.tensor_copy(lwork[:, 8:16], gates[:, t * E:(t + 1) * E]),
                "v", f"gstg_{t}")

    def mk_tileB(t):
        spar, swl = ring_next(ringS, f"gte_{t}")

        @mk("tensor")
        def _(e, t=t, spar=spar, swl=swl):
            if swl:
                w(e, swl)
            w(e, f"gstg_{t}")
            inc(e.transpose(psS[spar][0:E, 0:LANE], lwork[:, 8:16], identf[:]),
                "t", f"gttr_{t}")

        @mk("vector")
        def _(e, t=t, spar=spar):
            w(e, f"gttr_{t}")
            inc(e.tensor_copy(gateT[:, t * 128:(t + 1) * 128],
                              psS[spar][0:E, 0:LANE]), "v", f"gte_{t}")

    def mk_slot():
        @mk("vector")
        def _(e):
            w(e, f"gte_{TO_T - 1}")
            w(e, "ld_p0")
            inc(e.tensor_scalar(logitsT[:], gateT[:], 0.0, None, OP.is_gt),
                "v", "maskT")
            e.drain()
            zb = bass.broadcast_tensor_aps(consts[0:E, 3:4], logitsT[:])[0]
            e.tensor_tensor_scan(logitsT[:], logitsT[:], zb, 0.0, OP.add, OP.add)
            e.drain()
            e.tensor_scalar(logitsT[:], logitsT[:], float(CAP), bases_s[:],
                            OP.min, OP.add)
            e.drain()
            inc(e.scalar_tensor_tensor(logitsT[:], gateT[:], 0.0, logitsT[:],
                                       OP.is_gt, OP.mult), "v", "slotT")

    def mk_posm(j0):
        for j in range(j0, min(j0 + 2, QCH)):
            spar, swl = ring_next(ringS, f"posev_{j}")

            @mk("tensor")
            def _(e, j=j, spar=spar, swl=swl):
                if swl:
                    w(e, swl)
                w(e, "slotT")
                inc(e.matmul(psS[spar][0:1, 0:512], consts[0:E, 4:5],
                             logitsT[:, j * 512:(j + 1) * 512], start=True,
                             stop=True), "t", f"posm_{j}")

            @mk("vector")
            def _(e, j=j, spar=spar):
                w(e, f"posm_{j}")
                e.drain()
                inc(e.tensor_copy(logitsT[0:1, j * 512:(j + 1) * 512],
                                  psS[spar][0:1, 0:512]), "v", f"posev_{j}")

    def mk_postr(t0):
        for t in range(t0, min(t0 + 4, TO_T)):
            spar, swl = ring_next(ringS, f"posiev_{t}")

            @mk("tensor")
            def _(e, t=t, spar=spar, swl=swl):
                if swl:
                    w(e, swl)
                w(e, f"posev_{t // 4}")
                inc(e.transpose(psS[spar][0:LANE, 0:1],
                                logitsT[0:1, t * 128:(t + 1) * 128],
                                identf[0:1, 0:1]), "t", f"postr_{t}")

            @mk("vector")
            def _(e, t=t, spar=spar):
                w(e, f"postr_{t}")
                inc(e.tensor_copy(pos_i[:, t:t + 1], psS[spar][0:LANE, 0:1]),
                    "v", f"posiev_{t}")

        if t0 + 4 >= TO_T:
            @mk("vector")
            def _(e):
                M["pos_ready"] = M[f"posiev_{TO_T - 1}"]

    def mk_xdc(t0):
        for t in range(t0, min(t0 + 4, TO_T)):
            @mk("vector")
            def _(e, t=t):
                if t >= 2:
                    spar = t % 2
                    e.wait_ge(SEMS[f"sct{spar}"], 16 * (t // 2))
                inc(e.tensor_copy(xdt[t % 2][:].rearrange("p (hh ww) -> p hh ww",
                                                          ww=HD),
                                  xnt_heads(t)), "v", f"xdc_{t}")

            @mk("gpsimd")
            def _(e, t=t):
                w(e, f"xdc_{t}")
                if t == 0:
                    w(e, "pos_ready")
                lbl = f"sct_done{t % 2}" if t >= TO_T - 2 else None
                inc(e.indirect_dma_start(out=xg_h,
                                         out_offset=IOA(pos_i[:, t:t + 1], 0),
                                         in_=xdt[t % 2][:], in_offset=None),
                    f"sct{t % 2}", lbl, 16)

    def mk_b2(t0):
        for t in range(t0, min(t0 + 4, TO_T)):
            for dch in range(2):
                spar, swl = ring_next(ringS, f"b2e_{t}_{dch}")

                @mk("tensor")
                def _(e, t=t, dch=dch, spar=spar, swl=swl):
                    if swl:
                        w(e, swl)
                    w(e, f"gte_{t}")
                    w(e, "ld_b2")
                    inc(e.matmul(psS[spar][:, 0:512],
                                 gateT[:, t * 128:(t + 1) * 128],
                                 b2_s[:, dch * 512:(dch + 1) * 512], start=True,
                                 stop=True), "t", f"b2m_{t}_{dch}")

                @mk("vector")
                def _(e, t=t, dch=dch, spar=spar):
                    w(e, f"b2m_{t}_{dch}")
                    sl = acc[:, t * D + dch * 512: t * D + (dch + 1) * 512]
                    inc(e.tensor_tensor(sl, psS[spar][:, 0:512], sl, OP.add),
                        "v", f"b2e_{t}_{dch}")

    from functools import partial
    rider_groups = [[partial(mk_tileA, 0)]]
    for t in range(1, TO_T):
        rider_groups.append([partial(mk_tileB, t - 1), partial(mk_tileA, t)])
    rider_groups.append([partial(mk_tileB, TO_T - 1)])
    rider_groups.append([mk_slot])
    for j0 in range(0, QCH, 2):
        rider_groups.append([partial(mk_posm, j0)])
    for t0 in range(0, TO_T, 4):
        rider_groups.append([partial(mk_postr, t0)])
    for t0 in range(0, TO_T, 4):
        rider_groups.append([partial(mk_xdc, t0)])
    for t0 in range(0, TO_T, 4):
        rider_groups.append([partial(mk_b2, t0)])

    def prep_a(h):
        kpar = h % 3

        ktr_waits = []
        for kt in range(TF_T):
            ktr_waits.append(ring_next(ringX, f"kev_{h}_{kt}"))

        @mk("tensor")
        def _(e, h=h, ktr_waits=ktr_waits):
            for kt in range(TF_T):
                par, wl = ktr_waits[kt]
                if wl:
                    w(e, wl)
                w(e, f"xnt_{kt}")
                inc(e.transpose(psX[par][0:HD, 0:LANE], xnt_head(kt, h), ident[:]),
                    "t", f"ktr_{h}_{kt}")

        @mk("vector")
        def _(e, h=h, kpar=kpar, ktr_waits=ktr_waits):
            for kt in range(TF_T):
                w(e, f"ktr_{h}_{kt}")
                if kt == 0 and h >= 3:
                    w(e, f"sqq_{h - 3}_{QCH - 1}")
                    w(e, f"S_{h - 3}_{NPAIR - 1}_{TF_T - 1}")
                par = ktr_waits[kt][0]
                inc(e.tensor_copy(kbl[kpar][0:HD, kt * 128:(kt + 1) * 128],
                                  psX[par][0:HD, 0:LANE]), "v", f"kev_{h}_{kt}")
            e.drain()
            inc(e.tensor_scalar(qbl[kpar][0:HD, :], kbl[kpar][0:HD, 0:T_own],
                                cT[:, h:h + 1], None, OP.mult), "v", f"qbd_{h}")

    def prep_b(h):
        kpar = h % 3
        for j in range(KCH):
            @mk("scalar")
            def _(e, h=h, j=j, kpar=kpar):
                w(e, f"kev_{h}_{min(4 * j + 3, TF_T - 1)}")
                if j >= 2:
                    w(e, f"nkm_{h}_{j - 2}")
                elif h >= 1:
                    w(e, f"nqm_{h - 1}_{QCH - 1}")
                inc(e.activation(sql[j % 2][:], kbl[kpar][0:HD, j * 512:(j + 1) * 512],
                                 AF.Square), "s", f"sqk_{h}_{j}")

            spar, swl = ring_next(ringS, f"nkev_{h}_{j}")

            @mk("tensor")
            def _(e, h=h, j=j, spar=spar, swl=swl):
                w(e, f"sqk_{h}_{j}")
                if swl:
                    w(e, swl)
                inc(e.matmul(psS[spar][0:1, 0:512], ones64[:], sql[j % 2][:],
                             start=True, stop=True), "t", f"nkm_{h}_{j}")

            @mk("vector")
            def _(e, h=h, j=j, spar=spar):
                w(e, f"nkm_{h}_{j}")
                if j == 0 and h >= 1:
                    w(e, f"sqk2_{h - 1}")
                e.drain()
                e.tensor_reduce(asml[0:1, 1:2], psS[spar][0:1, 0:512], AX.X, OP.max)
                e.drain()
                if j == 0:
                    i1 = e.tensor_copy(asml[0:1, 0:1], asml[0:1, 1:2])
                else:
                    i1 = e.tensor_tensor(asml[0:1, 0:1], asml[0:1, 1:2],
                                         asml[0:1, 0:1], OP.max)
                inc(i1, "v", f"nkev_{h}_{j}")
                e.drain()

        for j in range(QCH):
            @mk("scalar")
            def _(e, h=h, j=j, kpar=kpar):
                w(e, f"qbd_{h}")
                if j >= 2:
                    w(e, f"nqm_{h}_{j - 2}")
                else:
                    w(e, f"nkm_{h}_{KCH - 1}")
                inc(e.activation(sql[j % 2][:], qbl[kpar][0:HD, j * 512:(j + 1) * 512],
                                 AF.Square), "s", f"sqq_{h}_{j}")

            spar, swl = ring_next(ringS, f"nqev_{h}_{j}")

            @mk("tensor")
            def _(e, h=h, j=j, spar=spar, swl=swl):
                w(e, f"sqq_{h}_{j}")
                if swl:
                    w(e, swl)
                inc(e.matmul(psS[spar][0:1, 0:512], ones64[:], sql[j % 2][:],
                             start=True, stop=True), "t", f"nqm_{h}_{j}")

            @mk("vector")
            def _(e, h=h, j=j, spar=spar):
                w(e, f"nqm_{h}_{j}")
                if j == 0 and h >= 1:
                    w(e, f"qm_{h - 1}")
                    w(e, f"sqq2_{h - 1}")
                e.drain()
                inc(e.tensor_copy(arow[0:1, j * 512:(j + 1) * 512],
                                  psS[spar][0:1, 0:512]), "v", f"nqev_{h}_{j}")

        @mk("scalar")
        def _(e, h=h):
            w(e, f"nkev_{h}_{KCH - 1}")
            if h >= 1:
                w(e, f"negm_{h - 1}")
            inc(e.activation(asml[0:1, 2:3], asml[0:1, 0:1], AF.Sqrt), "s", f"sqk2_{h}")
            w(e, f"nqev_{h}_{QCH - 1}")
            inc(e.activation(arow[0:1, 0:T_own], arow[0:1, 0:T_own], AF.Sqrt),
                "s", f"sqq2_{h}")

        @mk("vector")
        def _(e, h=h):
            w(e, f"sqq2_{h}")
            inc(e.tensor_scalar(arow[0:1, 0:T_own], arow[0:1, 0:T_own],
                                asml[0:1, 2:3], consts[0:1, 2:3], OP.mult, OP.mult),
                "v", f"negm_{h}")

        @mk("sync")
        def _(e, h=h, kpar=kpar):
            w(e, f"negm_{h}")
            if h >= 3:
                w(e, f"S_{h - 3}_{NPAIR - 1}_{TF_T - 1}")
            inc(e.dma_start(out=qbl[kpar][HD:WA, :], in_=arow[0:1, 0:T_own]),
                "qmh", f"qm_{h}", 16)

        @mk("gpsimd")
        def _(e, h=h, kpar=kpar):
            if h >= 3:
                w(e, f"S_{h - 3}_{NPAIR - 1}_{TF_T - 1}")
            elif h < 2:
                w(e, f"xnt_{TF_T - 1}")
                w(e, f"trdone_{TO_T - 1}")
            inc(e.memset(kbl[kpar][HD:WA, :], 1.0), "g", f"kone_{h}")

    def emit_ofin(hp, pp, otrE):
        q0p = pp * PB

        @mk("vector")
        def _(e, hp=hp, pp=pp, q0p=q0p, otrE=otrE):
            for j in range(JT):
                par = otrE[j][0]
                w(e, f"otr_{hp}_{pp}_{j}")
                tt = (q0p + j * 128) // 128
                e.drain()
                e.reciprocal(lw_recip[:], psX[par][:, HD:WA])
                e.drain()
                sl = acc[:, tt * D + hp * HD: tt * D + (hp + 1) * HD]
                inc(e.scalar_tensor_tensor(sl, psX[par][:, 0:HD], lw_recip[:],
                                           sl, OP.mult, OP.add),
                    "v", f"ofin_{hp}_{pp}_{j}")

    def emit_out_tail(hp, pp):
        oparp = (hp * NPAIR + pp) % 2
        otrE = [ring_next(ringX, f"ofin_{hp}_{pp}_{j}") for j in range(JT)]

        @mk("tensor")
        def _(e, hp=hp, pp=pp, oparp=oparp, otrE=otrE):
            for j in range(JT):
                half = j // 4 if NHALF > 1 else 0
                par, wl = otrE[j]
                w(e, f"oev_{hp}_{pp}_{half}")
                if wl:
                    w(e, wl)
                inc(e.transpose(psX[par][:, 0:WA],
                                obl[oparp][:, j * 128:(j + 1) * 128],
                                ident[0:WA, 0:WA]), "t", f"otr_{hp}_{pp}_{j}")

        emit_ofin(hp, pp, otrE)

    def main(h, p, riders=None, prev=None):
        kpar = h % 3
        q0 = p * PB
        pl = h * NPAIR + p
        opar = pl % 2

        s_entries = []
        for kt in range(TF_T):
            s_entries.append(ring_next(ringS, f"exp_{h}_{p}_{kt}"))

        KOFF = min(16, TF_T - JT)
        otrE = None
        if prev is not None:
            otrE = [ring_next(ringX, f"ofin_{prev[0]}_{prev[1]}_{j}")
                    for j in range(JT)]

        @mk("tensor")
        def _(e, h=h, p=p, q0=q0, kpar=kpar, pl=pl, s_entries=s_entries,
              prev=prev, otrE=otrE, KOFF=KOFF):
            for kt in range(TF_T):
                spar, swl = s_entries[kt]
                if swl:
                    w(e, swl)
                if kt == 0:
                    w(e, f"qm_{h}")
                    w(e, f"kone_{h}")
                    w(e, f"qbd_{h}")
                    w(e, f"kev_{h}_{TF_T - 1}")
                i1 = None
                for half in range(NHALF):
                    i1 = e.matmul(psS[spar][:, half * 512:(half + 1) * 512],
                                  kbl[kpar][:, kt * 128:(kt + 1) * 128],
                                  qbl[kpar][:, q0 + half * 512:q0 + (half + 1) * 512],
                                  start=True, stop=True)
                inc(i1, "t", f"S_{h}_{p}_{kt}")
                if kt >= 2:
                    ktp = kt - 2
                    w(e, f"exp_{h}_{p}_{ktp}")
                    for half in range(NHALF):
                        if ktp == 0 and pl >= 1:
                            pp = pl - 1
                            w(e, f"oev_{pp // NPAIR}_{pp % NPAIR}_{half}")
                        elif ktp == 0 and pl == 0 and half == 0:
                            w(e, f"lge_{TO_T - 1}")
                        i2 = e.matmul(psO[half][:, :], xnt_vaug(ktp, h),
                                      pbl[ktp % 4][:, half * 512:(half + 1) * 512],
                                      start=(ktp == 0), stop=False)
                        inc(i2, "t", f"pv_{h}_{p}_{ktp}_{half}")
                if otrE is not None and KOFF <= kt < KOFF + JT:
                    j = kt - KOFF
                    hp, pp = prev
                    oparp = (hp * NPAIR + pp) % 2
                    half = j // 4 if NHALF > 1 else 0
                    par, wl = otrE[j]
                    w(e, f"oev_{hp}_{pp}_{half}")
                    if wl:
                        w(e, wl)
                    inc(e.transpose(psX[par][:, 0:WA],
                                    obl[oparp][:, j * 128:(j + 1) * 128],
                                    ident[0:WA, 0:WA]), "t", f"otr_{hp}_{pp}_{j}")
            for ktp in (TF_T - 2, TF_T - 1):
                w(e, f"exp_{h}_{p}_{ktp}")
                for half in range(NHALF):
                    i2 = e.matmul(psO[half][:, :], xnt_vaug(ktp, h),
                                  pbl[ktp % 4][:, half * 512:(half + 1) * 512],
                                  start=False, stop=(ktp == TF_T - 1))
                    inc(i2, "t", f"pv_{h}_{p}_{ktp}_{half}")

        if otrE is not None:
            emit_ofin(prev[0], prev[1], otrE)

        @mk("scalar")
        def _(e, h=h, p=p, pl=pl, opar=opar, s_entries=s_entries):
            for kt in range(TF_T):
                spar = s_entries[kt][0]
                w(e, f"S_{h}_{p}_{kt}")
                if kt >= 4:
                    w(e, f"pv_{h}_{p}_{kt - 4}_{NHALF - 1}")
                elif p >= 1:
                    w(e, f"pv_{h}_{p - 1}_{TF_T - 4 + kt}_{NHALF - 1}")
                elif h >= 1:
                    w(e, f"pv_{h - 1}_{NPAIR - 1}_{TF_T - 4 + kt}_{NHALF - 1}")
                inc(e.activation(pbl[kt % 4][:], psS[spar][:], AF.Exp),
                    "s", f"exp_{h}_{p}_{kt}")
            for half in range(NHALF):
                w(e, f"pv_{h}_{p}_{TF_T - 1}_{half}")
                if pl >= 2:
                    pp = pl - 2
                    w(e, f"otr_{pp // NPAIR}_{pp % NPAIR}_{JT - 1}")
                inc(e.activation(obl[opar][:, half * 512:(half + 1) * 512],
                                 psO[half][:, :], AF.Copy), "s", f"oev_{h}_{p}_{half}")

        if riders:
            for th in riders:
                th()

    prep_a(0)
    prep_b(0)
    if H > 1:
        prep_a(1)
        prep_b(1)
    def mk_precast(idx):
        eidx, q = idx // NQ, idx % NQ

        @mk("gpsimd")
        def _(e, eidx=eidx, q=q):
            inc(e.dma_start(out=w1b_h[eidx, q], in_=w1_h[eidx, q]), "wc", None, 16)
            inc(e.dma_start(out=w2b_h[eidx, q], in_=w2_h[eidx, q]), "wc", None, 16)

    rq = list(rider_groups)
    prevp = None
    pcn = 0
    for h in range(H):
        for p in range(NPAIR):
            main(h, p, rq.pop(0) if rq else None, prev=prevp)
            prevp = (h, p)
            if pcn < NPH:
                mk_precast(pcn)
                pcn += 1
            if p == 0 and 2 <= h + 1 < H and h >= 1:
                prep_b(h + 1)
        if h + 2 < H:
            prep_a(h + 2)
    emit_out_tail(H - 1, NPAIR - 1)
    while rq:
        for th in rq.pop(0):
            th()
    while pcn < NPH:
        mk_precast(pcn)
        pcn += 1

    for cm in reversed(p3_cms):
        cm.__exit__(None, None, None)
    for cm in reversed(p3_pcms):
        cm.__exit__(None, None, None)
    for cm in reversed(p2keep_cms):
        cm.__exit__(None, None, None)
    xnt_cm.__exit__(None, None, None)

    assert CAP <= 512
    wq1_0, _ = sb("wq1_0", [LANE, DC * FQ], BF16)
    wq1_1, _ = sb("wq1_1", [LANE, DC * FQ], BF16)
    wq1_2, _ = sb("wq1_2", [LANE, DC * FQ], BF16)
    wq1 = [wq1_0, wq1_1, wq1_2]
    wq2_0, _ = sb("wq2_0", [LANE, FT_Q * D], BF16)
    wq2_1, _ = sb("wq2_1", [LANE, FT_Q * D], BF16)
    wq2_2, _ = sb("wq2_2", [LANE, FT_Q * D], BF16)
    wq2 = [wq2_0, wq2_1, wq2_2]
    hbuf, _ = sb("hbuf", [LANE, FT_Q * CAP], BF16)
    xgt_0, _ = sb("xgt_0", [LANE, DC * CAP], BF16)
    xgt_1, _ = sb("xgt_1", [LANE, DC * CAP], BF16)
    xgt = [xgt_0, xgt_1]
    yacc, _ = sb("yacc", [LANE, CT * D], F32)
    xstg_0, _ = sb("xstg_0", [LANE, D], BF16)
    xstg_1, _ = sb("xstg_1", [LANE, D], BF16)
    xstg = [xstg_0, xstg_1]

    psH0, _ = pst("psH0", [LANE, 512], F32)
    psH1, _ = pst("psH1", [LANE, 512], F32)
    psH = [psH0, psH1]
    psY0, _ = pst("psY0", [LANE, 512], F32)
    psY1, _ = pst("psY1", [LANE, 512], F32)
    psY = [psY0, psY1]
    psG0, _ = pst("psG0", [LANE, LANE], BF16)
    psG1, _ = pst("psG1", [LANE, LANE], BF16)
    psG2, _ = pst("psG2", [LANE, LANE], BF16)
    psG3, _ = pst("psG3", [LANE, LANE], BF16)
    psG = [psG0, psG1, psG2, psG3]

    def mk_wload(idx):
        eidx, q = idx // NQ, idx % NQ

        @mk("gpsimd")
        def _(e, idx=idx, eidx=eidx, q=q):
            par = idx % 3
            if idx <= 2:
                w(e, "sct_done0")
                w(e, "sct_done1")
                w(e, f"exp_{H - 1}_{NPAIR - 1}_{TF_T - 1}")
                w(e, f"oev_{H - 1}_{NPAIR - 1}_{NHALF - 1}")
                w(e, f"ofin_{H - 1}_{NPAIR - 1}_{JT - 1}")
                w(e, f"otr_{H - 1}_{NPAIR - 1}_{JT - 1}")
                w(e, f"b2e_{TO_T - 1}_{1}")
                e.wait_ge(SEMS["wc"], NPH * 32)
            if idx >= 3:
                w(e, f"ffn_w_done_{idx - 3}")
            inc(e.dma_start(out=wq1[par][:], in_=w1b_h[eidx, q]), f"ldw{par}", None, 16)
            inc(e.dma_start(out=wq2[par][:], in_=w2b_h[eidx, q]), f"ldw{par}", f"w12l_{idx}", 16)

    mk_wload(0)
    if NPH > 1:
        mk_wload(1)
    if NPH > 2:
        mk_wload(2)

    def ffn_stage(eidx):
        epar = eidx % 2
        for st in range(CT):
            @mk("sync")
            def _(e, eidx=eidx, st=st, epar=epar):
                g = eidx * CT + st
                spar = g % 2
                if eidx == 0 and st == 0:
                    w(e, "sct_done0")
                    w(e, "sct_done1")
                    w(e, f"ofin_{H - 1}_{NPAIR - 1}_{JT - 1}")
                    w(e, f"exp_{H - 1}_{NPAIR - 1}_{TF_T - 1}")
                    w(e, f"oev_{H - 1}_{NPAIR - 1}_{NHALF - 1}")
                    w(e, f"otr_{H - 1}_{NPAIR - 1}_{JT - 1}")
                    w(e, f"b2e_{TO_T - 1}_{1}")
                if g >= 2:
                    w(e, f"xgtr_{(g - 2) // CT}_{(g - 2) % CT}_{DC - 1}")
                r0 = eidx * CAP + st * 128
                inc(e.dma_start(out=xstg[spar][:], in_=xg_h[r0:r0 + 128, :]),
                    f"xg{spar}", f"xgl_{eidx}_{st}", 16)

            @mk("tensor")
            def _(e, eidx=eidx, st=st, epar=epar):
                g = eidx * CT + st
                spar = g % 2
                w(e, f"xgl_{eidx}_{st}")
                for c in range(DC):
                    u = g * DC + c
                    par = u % 4
                    if u >= 4:
                        w(e, f"xgte_{((u - 4) // DC) // CT}_{((u - 4) // DC) % CT}_{(u - 4) % DC}")
                    else:
                        w(e, f"ofin_{H - 1}_{NPAIR - 1}_{JT - 1}")
                    inc(e.transpose(psG[par][:], xstg[spar][:, c * 128:(c + 1) * 128],
                                    ident[:]), "t", f"xgtr_{eidx}_{st}_{c}")

            @mk("vector")
            def _(e, eidx=eidx, st=st, epar=epar):
                for c in range(DC):
                    u = (eidx * CT + st) * DC + c
                    w(e, f"xgtr_{eidx}_{st}_{c}")
                    if st == 0 and c == 0 and eidx >= 2:
                        w(e, f"h_{(eidx - 2) * NQ + NQ - 1}_{FT_Q - 1}")
                    inc(e.tensor_copy(
                        xgt[epar][:, c * CAP + st * 128: c * CAP + (st + 1) * 128],
                        psG[u % 4][:]), "v", f"xgte_{eidx}_{st}_{c}")

    ffn_stage(0)
    for eidx in range(E):
        epar = eidx % 2
        for q in range(NQ):
            idx = eidx * NQ + q
            par = idx % 3
            if idx + 3 < NPH:
                mk_wload(idx + 3)
            if q == 1 and eidx + 1 < E:
                ffn_stage(eidx + 1)

            for ft in range(FT_Q):
                @mk("tensor")
                def _(e, idx=idx, par=par, ft=ft, eidx=eidx, epar=epar):
                    hp = ft % 2
                    w(e, f"w12l_{idx}")
                    if ft >= 2:
                        w(e, f"gel_{idx}_{ft - 2}")
                    elif idx >= 1:
                        w(e, f"gel_{idx - 1}_{FT_Q - 2 + ft}")
                    if ft == 0:
                        w(e, f"xgte_{eidx}_{CT - 1}_{DC - 1}")
                    i1 = None
                    for c in range(DC):
                        i1 = e.matmul(psH[hp][:, 0:CAP],
                                      wq1[par][:, c * FQ + ft * 128:c * FQ + (ft + 1) * 128],
                                      xgt[epar][:, c * CAP:(c + 1) * CAP],
                                      start=(c == 0), stop=(c == DC - 1))
                    inc(i1, "t", f"h_{idx}_{ft}")

                @mk("scalar")
                def _(e, idx=idx, eidx=eidx, q=q, ft=ft):
                    hp = ft % 2
                    w(e, f"h_{idx}_{ft}")
                    if ft == 0 and idx >= 1:
                        w(e, f"y_{idx - 1}_{CT - 1}_1")
                    bcol = eidx * 32 + q * FT_Q + ft
                    inc(e.activation(hbuf[:, ft * CAP:(ft + 1) * CAP],
                                     psH[hp][:, 0:CAP], AF.Gelu,
                                     bias=b1_s[:, bcol:bcol + 1], scale=1.0),
                        "s", f"gel_{idx}_{ft}")

            for st in range(CT):
                for dch in range(2):
                    @mk("tensor")
                    def _(e, idx=idx, par=par, st=st, dch=dch):
                        u = st * 2 + dch
                        yp = u % 2
                        w(e, f"w12l_{idx}")
                        w(e, f"gel_{idx}_{FT_Q - 1}")
                        if u >= 2:
                            w(e, f"yev_{idx}_{(u - 2) // 2}_{(u - 2) % 2}")
                        elif idx >= 1:
                            w(e, f"yev_{idx - 1}_{CT - 1}_{u}")
                        i1 = None
                        for c in range(FT_Q):
                            i1 = e.matmul(psY[yp][:, :],
                                          hbuf[:, c * CAP + st * 128:c * CAP + (st + 1) * 128],
                                          wq2[par][:, c * D + dch * 512:c * D + (dch + 1) * 512],
                                          start=(c == 0), stop=(c == FT_Q - 1))
                        inc(i1, "t", f"y_{idx}_{st}_{dch}")
                        if st == CT - 1 and dch == 1:
                            M[f"ffn_w_done_{idx}"] = M[f"y_{idx}_{st}_{dch}"]

                    @mk("vector")
                    def _(e, idx=idx, eidx=eidx, q=q, st=st, dch=dch):
                        u = st * 2 + dch
                        yp = u % 2
                        w(e, f"y_{idx}_{st}_{dch}")
                        if q == 0 and u == 0 and eidx >= 1:
                            w(e, f"ysd_{eidx - 1}")
                        sl = yacc[:, st * D + dch * 512: st * D + (dch + 1) * 512]
                        if q == 0:
                            inc(e.tensor_copy(sl, psY[yp][:, :]), "v",
                                f"yev_{idx}_{st}_{dch}")
                        else:
                            inc(e.tensor_tensor(sl, psY[yp][:, :], sl, OP.add), "v",
                                f"yev_{idx}_{st}_{dch}")

        @mk("gpsimd")
        def _(e, eidx=eidx):
            idx = eidx * NQ + NQ - 1
            w(e, f"yev_{idx}_{CT - 1}_{1}")
            r0 = eidx * CAP
            lbl = "ys_all" if eidx == E - 1 else f"ysd_{eidx}"
            inc(e.dma_start(out=yscr_h[r0:r0 + CAP, :].rearrange("(t p) d -> p t d", p=LANE),
                            in_=yacc[:].rearrange("p (t d) -> p t d", d=D)),
                "ys", lbl, 16)
            if eidx == E - 1:
                M[f"ysd_{eidx}"] = M["ys_all"]

    for t in range(TO_T):
        @mk("gpsimd")
        def _(e, t=t):
            w(e, "ys_all")
            if t >= 2:
                w(e, f"gadd_{t - 2}")
            inc(e.indirect_dma_start(out=xstg[t % 2][:], out_offset=None,
                                     in_=yscr_h, in_offset=IOA(pos_i[:, t:t + 1], 0)),
                f"gt{t % 2}", f"gth_{t}", 16)

        @mk("vector")
        def _(e, t=t):
            w(e, f"gth_{t}")
            e.drain()
            e.tensor_reduce(lw_recip[:], gates[:, t * E:(t + 1) * E], AX.X, OP.add)
            e.drain()
            sl = acc[:, t * D:(t + 1) * D]
            inc(e.scalar_tensor_tensor(sl, xstg[t % 2][:], lw_recip[:], sl,
                                       OP.mult, OP.add), "v", f"gadd_{t}")

    @mk("sync")
    def _(e):
        w(e, f"gadd_{TO_T - 1}")
        inc(e.dma_start(out=out_h.rearrange("(t p) d -> p t d", p=LANE),
                        in_=acc[:].rearrange("p (t d) -> p t d", d=D)),
            "o", "out_done", 16)
        e.wait_ge(SEMS["o"], 16)

    C.clear()
    mock = _MockEng()
    for eng, fn in chunks:
        fn(mock)
    C.clear()
    strict[0] = True
    blk_cm = nc.Block()
    block = blk_cm.__enter__()
    for engname in ["sync", "gpsimd", "vector", "scalar", "tensor"]:
        def section(handle, engname=engname):
            for eng, fn in chunks:
                if eng == engname:
                    fn(handle)
        getattr(block, engname)(section)
    blk_cm.__exit__(None, None, None)
    return nc


def _host_layouts(W1, b1, W2, b2, Wr, br, rot, ent):
    w1 = np.ascontiguousarray(
        W1.reshape(E, DC, LANE, NQ, FQ).transpose(0, 3, 2, 1, 4)
    ).reshape(E, NQ, LANE, DC * FQ)
    w2 = np.ascontiguousarray(
        W2.reshape(E, NQ, FT_Q, LANE, D).transpose(0, 1, 3, 2, 4)
    ).reshape(E, NQ, LANE, FT_Q * D)
    b1l = np.ascontiguousarray(b1.reshape(E, 32, LANE).transpose(2, 0, 1)).reshape(LANE, E * 32)
    wr = np.ascontiguousarray(Wr.reshape(DC, LANE, E).transpose(1, 0, 2)).reshape(LANE, DC * E)
    brl = np.ascontiguousarray(br.reshape(E, 1))
    rotT = np.ascontiguousarray(rot.T)
    entT = np.ascontiguousarray(ent.T)
    return dict(w1=w1, w2=w2, b1=b1l, b2=np.ascontiguousarray(b2), wr=wr, br=brl,
                rotT=rotT, entT=entT)


def kernel(L0_attn_inputs, L0_attn_rotation_params, L0_attn_entangle_params,
           L0_ffn_W1, L0_ffn_b1, L0_ffn_W2, L0_ffn_b2, L0_ffn_Wr, L0_ffn_br):
    x = np.ascontiguousarray(np.asarray(L0_attn_inputs, dtype=np.float32))
    B, T, _ = x.shape
    T_own = T // 2
    params = _host_layouts(np.asarray(L0_ffn_W1, np.float32),
                           np.asarray(L0_ffn_b1, np.float32),
                           np.asarray(L0_ffn_W2, np.float32),
                           np.asarray(L0_ffn_b2, np.float32),
                           np.asarray(L0_ffn_Wr, np.float32),
                           np.asarray(L0_ffn_br, np.float32),
                           np.asarray(L0_attn_rotation_params, np.float32),
                           np.asarray(L0_attn_entangle_params, np.float32))
    nc = build_program(T, T_own)
    cap = T_own // 4
    params["xg"] = np.zeros((E * cap, D), dtype=ml_dtypes.bfloat16)
    params["bases"] = (np.arange(E, dtype=np.float32) * cap - 1.0).reshape(E, 1)
    in_maps = []
    for core in range(8):
        b, half = core // 2, core % 2
        if half == 0:
            xr = x[b]
        else:
            xr = np.ascontiguousarray(np.concatenate([x[b, T_own:], x[b, :T_own]], axis=0))
        m = dict(params)
        m["xf"] = xr
        in_maps.append(m)
    res = run_bass_kernel_spmd(nc, in_maps, core_ids=list(range(8)))
    out = np.empty((B, T, D), np.float32)
    for core in range(8):
        b, half = core // 2, core % 2
        sl = slice(0, T_own) if half == 0 else slice(T_own, T)
        out[b, sl] = res.results[core]["out"]
    return out


# version 28 (final)
# speedup vs baseline: 1.0133x; 1.0133x over previous
import numpy as np
import ml_dtypes

import concourse.bass as bass
import concourse.mybir as mybir
from concourse.alu_op_type import AluOpType as OP
from concourse.bass_utils import run_bass_kernel_spmd

F32 = mybir.dt.float32
BF16 = mybir.dt.bfloat16
AF = mybir.ActivationFunctionType
AX = mybir.AxisListType

D = 1024
DC = 8
H = 16
HD = 64
E = 8
F = 4096
NQ = 4
FQ = F // NQ
FT_Q = FQ // 128
LANE = 128
WA = HD + 1
W_AUG = H * WA


class _MockInst:
    def then_inc(self, *a, **k):
        return self


class _MockEng:
    def wait_ge(self, *a, **k):
        pass

    def __getattr__(self, name):
        return lambda *a, **k: _MockInst()


def build_program(T_full, T_own, pb_cap=1024):
    TF_T = T_full // LANE
    TO_T = T_own // LANE
    KCH = T_full // 512
    QCH = T_own // 512
    PB = min(pb_cap, T_own)
    NPAIR = T_own // PB
    NHALF = PB // 512
    JT = PB // 128
    NPH = E * NQ
    CAP = T_own // 4
    CT = CAP // 128
    NSLOT = E * CAP

    nc = bass.Bass("TRN2", target_bir_lowering=False)

    xf = nc.dram_tensor("xf", [T_full, D], F32, kind="ExternalInput").ap()
    rotT_h = nc.dram_tensor("rotT", [HD, H], F32, kind="ExternalInput").ap()
    entT_h = nc.dram_tensor("entT", [HD, H], F32, kind="ExternalInput").ap()
    wr_h = nc.dram_tensor("wr", [LANE, DC * E], F32, kind="ExternalInput").ap()
    br_h = nc.dram_tensor("br", [E, 1], F32, kind="ExternalInput").ap()
    b1_h = nc.dram_tensor("b1", [LANE, E * 32], F32, kind="ExternalInput").ap()
    b2_h = nc.dram_tensor("b2", [E, D], F32, kind="ExternalInput").ap()
    w1_h = nc.dram_tensor("w1", [E, NQ, LANE, DC * FQ], F32, kind="ExternalInput").ap()
    w2_h = nc.dram_tensor("w2", [E, NQ, LANE, FT_Q * D], F32, kind="ExternalInput").ap()
    xg_h = nc.dram_tensor("xg", [NSLOT, D], BF16, kind="ExternalInput").ap()
    bases_h = nc.dram_tensor("bases", [E, 1], F32, kind="ExternalInput").ap()
    yscr_h = nc.dram_tensor("yscr", [NSLOT, D], BF16, kind="Internal").ap()
    w1b_h = nc.dram_tensor("w1b", [E, NQ, LANE, DC * FQ], BF16, kind="Internal").ap()
    w2b_h = nc.dram_tensor("w2b", [E, NQ, LANE, FT_Q * D], BF16, kind="Internal").ap()
    out_h = nc.dram_tensor("out", [T_own, D], F32, kind="ExternalOutput").ap()

    def sb(name, shape, dt):
        cm = nc.sbuf_tensor(name, shape, dt)
        return cm.__enter__(), cm

    def pst(name, shape, dt):
        cm = nc.psum_tensor(name, shape, dt)
        return cm.__enter__(), cm

    acc, _ = sb("acc", [LANE, TO_T * D], F32)
    consts, _ = sb("consts", [LANE, 6], F32)
    cT, _ = sb("cT", [HD, H], F32)
    rot_s, _ = sb("rot_s", [HD, H], F32)
    ent_s, _ = sb("ent_s", [HD, H], F32)
    wr_s, _ = sb("wr_s", [LANE, DC * E], F32)
    wr_hi, _ = sb("wr_hi", [LANE, DC * E], BF16)
    wr_lo, _ = sb("wr_lo", [LANE, DC * E], BF16)
    br_s, _ = sb("br_s", [E, 1], F32)
    b1_s, _ = sb("b1_s", [LANE, E * 32], F32)
    ident, _ = sb("ident", [LANE, LANE], BF16)
    identf, _ = sb("identf", [LANE, LANE], F32)
    ones64, _ = sb("ones64", [HD, 1], BF16)
    gates, _ = sb("gates", [LANE, TO_T * E], F32)
    lw_recip, _ = sb("lw_recip", [LANE, 1], F32)
    bases_s, _ = sb("bases_s", [E, 1], F32)
    pos_i, _ = sb("pos_i", [LANE, TO_T], mybir.dt.int32)

    xnt, xnt_cm = sb("xnt", [LANE, TF_T * W_AUG], BF16)

    logitsT, k1 = sb("logitsT", [E, T_own], F32)
    lwork, k2 = sb("lwork", [LANE, 24], F32)
    gateT, k3 = sb("gateT", [E, T_own], BF16)
    b2_s, k4 = sb("b2_s", [E, D], BF16)
    p2keep_cms = [k1, k2, k3, k4]

    lntmp, c1 = sb("lntmp", [LANE, D], F32)
    hitmp, c2 = sb("hitmp", [LANE, D], BF16)
    lotmp, c3 = sb("lotmp", [LANE, D], BF16)
    sqscr, c4 = sb("sqscr", [LANE, D], BF16)
    stat, c5 = sb("stat", [LANE, 16], F32)
    xst0, c6 = sb("xstage0", [LANE, D], F32)
    xst1, c7 = sb("xstage1", [LANE, D], F32)
    xst = [xst0, xst1]
    hiT, c8 = sb("hiT", [LANE, D], BF16)
    loT, c9 = sb("loT", [LANE, D], BF16)
    p12_cms = [c1, c2, c3, c4, c5, c6, c7, c8, c9]

    ps_tr0, d1 = pst("ps_tr0", [LANE, LANE], BF16)
    ps_tr1, d2 = pst("ps_tr1", [LANE, LANE], BF16)
    ps_tr2, d1b = pst("ps_tr2", [LANE, LANE], BF16)
    ps_tr3, d2b = pst("ps_tr3", [LANE, LANE], BF16)
    ps_tr = [ps_tr0, ps_tr1, ps_tr2, ps_tr3]
    ps_lg0, d3 = pst("ps_lg0", [E, LANE], F32)
    ps_lg1, d4 = pst("ps_lg1", [E, LANE], F32)
    ps_lg = [ps_lg0, ps_lg1]
    p12_pcms = [d1, d2, d1b, d2b, d3, d4]

    SEMS = {}
    for n in ["ldx", "ldx0", "ldx1", "qmh", "ldw0", "ldw1", "ldb", "g", "t", "s", "v", "o",
              "ldw2", "sct0", "sct1", "xg0", "xg1", "ys", "gt0", "gt1", "wc"]:
        cm = nc.semaphore(n)
        SEMS[n] = cm.__enter__()

    chunks = []
    M = {}
    C = {}
    strict = [False]

    def mk(engname):
        def adder(fn):
            chunks.append((engname, fn))
        return adder

    def inc(inst, semname, label=None, k=1):
        C[semname] = C.get(semname, 0) + k
        if label is not None:
            if label in M:
                assert M[label] == (semname, C[semname]), (
                    label, M[label], (semname, C[semname]))
            M[label] = (semname, C[semname])
        if not isinstance(inst, _MockInst):
            inst.then_inc(SEMS[semname], k)

    def w(e, label):
        if label not in M:
            assert not strict[0], f"forward ref: {label}"
            return
        s, vv = M[label]
        e.wait_ge(SEMS[s], vv)

    def xnt_tile3(t):
        return xnt[:, t * W_AUG:(t + 1) * W_AUG].rearrange("p (hh ww) -> p hh ww", ww=WA)

    def xnt_heads(t):
        return xnt_tile3(t)[:, :, 0:HD]

    def xnt_head(t, h):
        return xnt[:, t * W_AUG + h * WA: t * W_AUG + h * WA + HD]

    def xnt_vaug(t, h):
        return xnt[:, t * W_AUG + h * WA: t * W_AUG + (h + 1) * WA]

    def hitmp3():
        return hitmp[:].rearrange("p (hh ww) -> p hh ww", ww=HD)

    @mk("sync")
    def _(e):
        inc(e.dma_start(out=acc[:].rearrange("p (t d) -> p t d", d=D),
                        in_=xf[0:T_own, :].rearrange("(t p) d -> p t d", p=LANE)),
            "ldx", None, 16)
        inc(e.dma_start(out=rot_s[:], in_=rotT_h[:]), "ldx", None, 16)
        inc(e.dma_start(out=ent_s[:], in_=entT_h[:]), "ldx", None, 16)
        inc(e.dma_start(out=wr_s[:], in_=wr_h[:]), "ldx", None, 16)
        inc(e.dma_start(out=br_s[:], in_=br_h[:]), "ldx", None, 16)
        inc(e.dma_start(out=br_s[:], in_=br_h[:]), "ldx", None, 16) if False else None
        inc(e.dma_start(out=bases_s[:], in_=bases_h[:]), "ldx", None, 16)
        inc(e.dma_start(out=b1_s[:], in_=b1_h[:]), "ldx", "ld_p0", 16)

    @mk("gpsimd")
    def _(e):
        inc(e.dma_start(out=b2_s[:], in_=b2_h[:]), "ldb", "ld_b2", 16)
        e.memset(ident[:], 0.0)
        e.memset(identf[:], 0.0)
        e.drain()
        e.affine_select(out=ident[:], in_=ident[:], compare_op=OP.not_equal,
                        fill=1.0, base=0, pattern=[[-1, LANE]], channel_multiplier=1)
        e.affine_select(out=identf[:], in_=identf[:], compare_op=OP.not_equal,
                        fill=1.0, base=0, pattern=[[-1, LANE]], channel_multiplier=1)
        e.memset(ones64[:], 1.0)
        last = None
        for t in range(TF_T):
            last = e.memset(xnt_tile3(t)[:, :, HD:WA], 1.0)
        inc(last, "g", "g_const")

    @mk("vector")
    def _(e):
        e.memset(consts[:, 0:1], 1.0 / D)
        e.memset(consts[:, 1:2], 1e-5)
        e.memset(consts[:, 2:3], -1.0)
        e.memset(consts[:, 3:4], 0.0)
        e.memset(consts[:, 4:5], 1.0)
        w(e, "ld_p0")
        e.tensor_tensor(cT[:], rot_s[:], ent_s[:], OP.mult)
        e.drain()
        e.tensor_scalar(cT[:], cT[:], 0.125, None, OP.mult)
        e.tensor_copy(wr_hi[:], wr_s[:])
        e.drain()
        inc(e.tensor_tensor(wr_lo[:], wr_s[:], wr_hi[:], OP.subtract), "v", "v_const")

    def p1_A(t):
        own = t < TO_T
        i = t - TO_T
        so = (t % 2) * 8

        if not own:
            @mk("sync")
            def _(e, t=t, i=i):
                if i >= 2:
                    w(e, f"xnt_{t - 2}")
                inc(e.dma_start(out=xst[i % 2][:], in_=xf[t * LANE:(t + 1) * LANE, :]),
                    f"ldx{i % 2}", f"xoth_{t}", 16)

        @mk("vector")
        def _(e, t=t, own=own, i=i, so=so):
            if own:
                if t == 0:
                    w(e, "ld_p0")
                srcap = acc[:, t * D:(t + 1) * D]
            else:
                w(e, f"xoth_{t}")
                srcap = xst[i % 2][:]
            inc(e.tensor_reduce(stat[:, so:so + 1], srcap, AX.X, OP.add),
                "v", f"sum_{t}")

        @mk("scalar")
        def _(e, t=t, own=own, i=i, so=so):
            w(e, f"sum_{t}")
            srcap = acc[:, t * D:(t + 1) * D] if own else xst[i % 2][:]
            e.drain()
            inc(e.activation(sqscr[:], srcap, AF.Square,
                             accum_out=stat[:, so + 1:so + 2]), "s", f"sqs_{t}")

    def p1_B(t):
        own = t < TO_T
        i = t - TO_T
        so = (t % 2) * 8

        @mk("vector")
        def _(e, t=t, so=so):
            w(e, f"sqs_{t}")
            e.tensor_scalar(stat[:, so + 2:so + 3], stat[:, so:so + 1],
                            consts[:, 0:1], None, OP.mult)
            e.tensor_scalar(stat[:, so + 3:so + 4], stat[:, so + 1:so + 2],
                            consts[:, 0:1], None, OP.mult)
            e.drain()
            e.tensor_tensor(stat[:, so + 4:so + 5], stat[:, so + 2:so + 3],
                            stat[:, so + 2:so + 3], OP.mult)
            e.drain()
            inc(e.tensor_tensor(stat[:, so + 4:so + 5], stat[:, so + 3:so + 4],
                                stat[:, so + 4:so + 5], OP.subtract), "v", f"var_{t}")

        @mk("scalar")
        def _(e, t=t, so=so):
            w(e, f"var_{t}")
            inc(e.activation(stat[:, so + 5:so + 6], stat[:, so + 4:so + 5], AF.Sqrt,
                             bias=consts[:, 1:2], scale=1.0), "s", f"sd_{t}")

        if own:
            @mk("vector")
            def _(e, t=t, so=so):
                w(e, f"sd_{t}")
                e.reciprocal(stat[:, so + 6:so + 7], stat[:, so + 5:so + 6])
                e.drain()
                e.tensor_scalar(lntmp[:], acc[:, t * D:(t + 1) * D],
                                stat[:, so + 2:so + 3], stat[:, so + 6:so + 7],
                                OP.subtract, OP.mult)
                e.drain()
                if t > 0:
                    w(e, f"trdone_{t - 1}")
                e.tensor_copy(hitmp[:], lntmp[:])
                e.drain()
                w(e, "g_const")
                inc(e.tensor_copy(xnt_heads(t), hitmp3()), "v", f"xnt_{t}")
                inc(e.tensor_tensor(lotmp[:], lntmp[:], hitmp[:], OP.subtract),
                    "v", f"ln_{t}")

            @mk("tensor")
            def _(e, t=t):
                w(e, f"ln_{t}")
                for c in range(DC):
                    hslot = (c % 2) * 2
                    lslot = (c % 2) * 2 + 1
                    if c > 1:
                        w(e, f"evh_{t}_{c - 2}")
                    elif t > 0:
                        w(e, f"evh_{t - 1}_{DC - 2 + c}")
                    inc(e.transpose(ps_tr[hslot][:], hitmp[:, c * 128:(c + 1) * 128],
                                    ident[:]), "t", f"trh_{t}_{c}")
                    if c > 1:
                        w(e, f"evl_{t}_{c - 2}")
                    elif t > 0:
                        w(e, f"evl_{t - 1}_{DC - 2 + c}")
                    inc(e.transpose(ps_tr[lslot][:], lotmp[:, c * 128:(c + 1) * 128],
                                    ident[:]), "t", f"trl_{t}_{c}")
                    if c == DC - 1:
                        M[f"trdone_{t}"] = M[f"trl_{t}_{c}"]

            @mk("vector")
            def _(e, t=t):
                for c in range(DC):
                    hslot = (c % 2) * 2
                    lslot = (c % 2) * 2 + 1
                    w(e, f"trh_{t}_{c}")
                    if t > 0 and c == 0:
                        w(e, f"lgt_{t - 1}")
                    inc(e.tensor_copy(hiT[:, c * 128:(c + 1) * 128], ps_tr[hslot][:]),
                        "v", f"evh_{t}_{c}")
                    w(e, f"trl_{t}_{c}")
                    inc(e.tensor_copy(loT[:, c * 128:(c + 1) * 128], ps_tr[lslot][:]),
                        "v", f"evl_{t}_{c}")

            @mk("tensor")
            def _(e, t=t):
                par = t % 2
                if t >= 2:
                    w(e, f"lge_{t - 2}")
                w(e, "v_const")
                i1 = None
                for c in range(DC):
                    w(e, f"evh_{t}_{c}")
                    w(e, f"evl_{t}_{c}")
                    hT = hiT[:, c * 128:(c + 1) * 128]
                    lT = loT[:, c * 128:(c + 1) * 128]
                    i1 = e.matmul(ps_lg[par][:], wr_hi[:, c * E:(c + 1) * E], hT,
                                  start=(c == 0), stop=False)
                    i1 = e.matmul(ps_lg[par][:], wr_lo[:, c * E:(c + 1) * E], hT,
                                  start=False, stop=False)
                    i1 = e.matmul(ps_lg[par][:], wr_hi[:, c * E:(c + 1) * E], lT,
                                  start=False, stop=(c == DC - 1))
                inc(i1, "t", f"lgt_{t}")

            @mk("vector")
            def _(e, t=t):
                w(e, f"lgt_{t}")
                inc(e.tensor_scalar(logitsT[:, t * 128:(t + 1) * 128], ps_lg[t % 2][:],
                                    br_s[:], None, OP.add), "v", f"lge_{t}")
        else:
            @mk("vector")
            def _(e, t=t, i=i, so=so):
                w(e, f"sd_{t}")
                e.reciprocal(stat[:, so + 6:so + 7], stat[:, so + 5:so + 6])
                e.drain()
                w(e, "g_const")
                inc(e.tensor_scalar(xnt_heads(t), xst[i % 2][:],
                                    stat[:, so + 2:so + 3], stat[:, so + 6:so + 7],
                                    OP.subtract, OP.mult), "v", f"xnt_{t}")

    p1_A(0)
    if TF_T > 1:
        p1_A(1)
    for t in range(TF_T):
        p1_B(t)
        if t + 2 < TF_T:
            p1_A(t + 2)

    IOA = bass.IndirectOffsetOnAxis

    for cm in reversed(p12_cms):
        cm.__exit__(None, None, None)
    for cm in reversed(p12_pcms):
        cm.__exit__(None, None, None)

    kb0, e1 = sb("kb0", [WA, T_full], BF16)
    kb1, e2 = sb("kb1", [WA, T_full], BF16)
    kb2, e1b = sb("kb2", [WA, T_full], BF16)
    kbl = [kb0, kb1, kb2]
    qb0, e3 = sb("qb0", [WA, T_own], BF16)
    qb1, e4 = sb("qb1", [WA, T_own], BF16)
    qb2, e3b = sb("qb2", [WA, T_own], BF16)
    qbl = [qb0, qb1, qb2]
    sq0, e5 = sb("sq0", [HD, 512], BF16)
    sq1, e6 = sb("sq1", [HD, 512], BF16)
    sql = [sq0, sq1]
    pb0, e7 = sb("pb0", [LANE, PB], BF16)
    pb2, e7b = sb("pb2", [LANE, PB], BF16)
    pb3, e7c = sb("pb3", [LANE, PB], BF16)
    pb1, e8 = sb("pb1", [LANE, PB], BF16)
    pbl = [pb0, pb1, pb2, pb3]
    ob0, e9 = sb("ob0", [WA, PB], BF16)
    ob1, e10 = sb("ob1", [WA, PB], BF16)
    obl = [ob0, ob1]
    arow, e11 = sb("arow", [1, T_own], BF16)
    asml, e12 = sb("asml", [1, 4], F32)
    xdt0, e15 = sb("xdt0", [LANE, D], BF16)
    xdt1, e16 = sb("xdt1", [LANE, D], BF16)
    xdt = [xdt0, xdt1]
    p3_cms = [e1, e2, e1b, e3, e4, e3b, e5, e6, e7, e7b, e7c, e8, e9, e10, e11, e12, e15, e16]
    if PB >= D:
        hit2 = [pb0[:, 0:D], pb1[:, 0:D]]
    else:
        h2a, e13 = sb("hit2_0", [LANE, D], BF16)
        h2b, e14 = sb("hit2_1", [LANE, D], BF16)
        hit2 = [h2a[:], h2b[:]]
        p3_cms += [e13, e14]

    psS0, f1 = pst("psS0", [LANE, PB], F32)
    psS1, f2 = pst("psS1", [LANE, PB], F32)
    psS = [psS0, psS1]
    psO0, f3 = pst("psO0", [WA, 512], F32)
    psO1, f4 = pst("psO1", [WA, 512], F32)
    psO = [psO0, psO1]
    psX0, f5 = pst("psX0", [LANE, 1024], BF16)
    psX1, f6 = pst("psX1", [LANE, 1024], BF16)
    psX = [psX0, psX1]
    psX2 = psX
    p3_pcms = [f1, f2, f3, f4, f5, f6]

    ringX = []
    ringS = []

    def ring_next(rg, consumer_label):
        i = len(rg)
        waitlbl = rg[i - 2] if i >= 2 else f"lge_{TO_T - 1}"
        rg.append(consumer_label)
        return i % 2, waitlbl

    def mk_tileA(t):
        spar, swl = ring_next(ringS, f"lmax_{t}")

        @mk("tensor")
        def _(e, t=t, spar=spar, swl=swl):
            if swl:
                w(e, swl)
            w(e, f"lge_{t}")
            inc(e.transpose(psS[spar][0:LANE, 0:E], logitsT[:, t * 128:(t + 1) * 128],
                            identf[0:E, 0:E]), "t", f"lgtr_{t}")

        @mk("vector")
        def _(e, t=t, spar=spar):
            w(e, f"lgtr_{t}")
            if t >= 1:
                w(e, f"gttr_{t - 1}")
            e.tensor_copy(lwork[:, 0:E], psS[spar][0:LANE, 0:E])
            e.drain()
            inc(e.tensor_reduce(lwork[:, 8:9], lwork[:, 0:E], AX.X, OP.max,
                                negate=True), "v", f"lmax_{t}")

        @mk("scalar")
        def _(e, t=t):
            w(e, f"lmax_{t}")
            inc(e.activation(lwork[:, 16:24], lwork[:, 0:E], AF.Exp,
                             bias=lwork[:, 8:9], scale=1.0,
                             accum_out=lwork[:, 9:10]), "s", f"gexp_{t}")

        @mk("vector")
        def _(e, t=t):
            w(e, f"gexp_{t}")
            e.reciprocal(lwork[:, 10:11], lwork[:, 9:10])
            e.tensor_scalar(lwork[:, 11:12], lwork[:, 8:9], consts[:, 2:3], None,
                            OP.mult)
            e.drain()
            bc = bass.broadcast_tensor_aps(lwork[:, 11:12], lwork[:, 0:E])[0]
            e.tensor_tensor(gates[:, t * E:(t + 1) * E], lwork[:, 0:E], bc,
                            OP.is_equal)
            e.drain()
            inc(e.tensor_scalar(gates[:, t * E:(t + 1) * E],
                                gates[:, t * E:(t + 1) * E], lwork[:, 10:11],
                                None, OP.mult), "v", f"gts_{t}")
            e.drain()
            inc(e.tensor_copy(lwork[:, 8:16], gates[:, t * E:(t + 1) * E]),
                "v", f"gstg_{t}")

    def mk_tileB(t):
        spar, swl = ring_next(ringS, f"gte_{t}")

        @mk("tensor")
        def _(e, t=t, spar=spar, swl=swl):
            if swl:
                w(e, swl)
            w(e, f"gstg_{t}")
            inc(e.transpose(psS[spar][0:E, 0:LANE], lwork[:, 8:16], identf[:]),
                "t", f"gttr_{t}")

        @mk("vector")
        def _(e, t=t, spar=spar):
            w(e, f"gttr_{t}")
            inc(e.tensor_copy(gateT[:, t * 128:(t + 1) * 128],
                              psS[spar][0:E, 0:LANE]), "v", f"gte_{t}")

    def mk_slot():
        @mk("vector")
        def _(e):
            w(e, f"gte_{TO_T - 1}")
            w(e, "ld_p0")
            inc(e.tensor_scalar(logitsT[:], gateT[:], 0.0, None, OP.is_gt),
                "v", "maskT")
            e.drain()
            zb = bass.broadcast_tensor_aps(consts[0:E, 3:4], logitsT[:])[0]
            e.tensor_tensor_scan(logitsT[:], logitsT[:], zb, 0.0, OP.add, OP.add)
            e.drain()
            e.tensor_scalar(logitsT[:], logitsT[:], float(CAP), bases_s[:],
                            OP.min, OP.add)
            e.drain()
            inc(e.scalar_tensor_tensor(logitsT[:], gateT[:], 0.0, logitsT[:],
                                       OP.is_gt, OP.mult), "v", "slotT")

    def mk_posm(j0):
        for j in range(j0, min(j0 + 2, QCH)):
            spar, swl = ring_next(ringS, f"posev_{j}")

            @mk("tensor")
            def _(e, j=j, spar=spar, swl=swl):
                if swl:
                    w(e, swl)
                w(e, "slotT")
                inc(e.matmul(psS[spar][0:1, 0:512], consts[0:E, 4:5],
                             logitsT[:, j * 512:(j + 1) * 512], start=True,
                             stop=True), "t", f"posm_{j}")

            @mk("vector")
            def _(e, j=j, spar=spar):
                w(e, f"posm_{j}")
                e.drain()
                inc(e.tensor_copy(logitsT[0:1, j * 512:(j + 1) * 512],
                                  psS[spar][0:1, 0:512]), "v", f"posev_{j}")

    def mk_postr(t0):
        for t in range(t0, min(t0 + 4, TO_T)):
            spar, swl = ring_next(ringS, f"posiev_{t}")

            @mk("tensor")
            def _(e, t=t, spar=spar, swl=swl):
                if swl:
                    w(e, swl)
                w(e, f"posev_{t // 4}")
                inc(e.transpose(psS[spar][0:LANE, 0:1],
                                logitsT[0:1, t * 128:(t + 1) * 128],
                                identf[0:1, 0:1]), "t", f"postr_{t}")

            @mk("vector")
            def _(e, t=t, spar=spar):
                w(e, f"postr_{t}")
                inc(e.tensor_copy(pos_i[:, t:t + 1], psS[spar][0:LANE, 0:1]),
                    "v", f"posiev_{t}")

        if t0 + 4 >= TO_T:
            @mk("vector")
            def _(e):
                M["pos_ready"] = M[f"posiev_{TO_T - 1}"]

    def mk_xdc(t0):
        for t in range(t0, min(t0 + 4, TO_T)):
            @mk("vector")
            def _(e, t=t):
                if t >= 2:
                    spar = t % 2
                    e.wait_ge(SEMS[f"sct{spar}"], 16 * (t // 2))
                inc(e.tensor_copy(xdt[t % 2][:].rearrange("p (hh ww) -> p hh ww",
                                                          ww=HD),
                                  xnt_heads(t)), "v", f"xdc_{t}")

            @mk("gpsimd")
            def _(e, t=t):
                w(e, f"xdc_{t}")
                if t == 0:
                    w(e, "pos_ready")
                lbl = f"sct_done{t % 2}" if t >= TO_T - 2 else None
                inc(e.indirect_dma_start(out=xg_h,
                                         out_offset=IOA(pos_i[:, t:t + 1], 0),
                                         in_=xdt[t % 2][:], in_offset=None),
                    f"sct{t % 2}", lbl, 16)

    def mk_b2(t0):
        for t in range(t0, min(t0 + 4, TO_T)):
            for dch in range(2):
                spar, swl = ring_next(ringS, f"b2e_{t}_{dch}")

                @mk("tensor")
                def _(e, t=t, dch=dch, spar=spar, swl=swl):
                    if swl:
                        w(e, swl)
                    w(e, f"gte_{t}")
                    w(e, "ld_b2")
                    inc(e.matmul(psS[spar][:, 0:512],
                                 gateT[:, t * 128:(t + 1) * 128],
                                 b2_s[:, dch * 512:(dch + 1) * 512], start=True,
                                 stop=True), "t", f"b2m_{t}_{dch}")

                @mk("vector")
                def _(e, t=t, dch=dch, spar=spar):
                    w(e, f"b2m_{t}_{dch}")
                    sl = acc[:, t * D + dch * 512: t * D + (dch + 1) * 512]
                    inc(e.tensor_tensor(sl, psS[spar][:, 0:512], sl, OP.add),
                        "v", f"b2e_{t}_{dch}")

    from functools import partial
    rider_groups = [[partial(mk_tileA, 0)]]
    for t in range(1, TO_T):
        rider_groups.append([partial(mk_tileB, t - 1), partial(mk_tileA, t)])
    rider_groups.append([partial(mk_tileB, TO_T - 1)])
    rider_groups.append([mk_slot])
    for j0 in range(0, QCH, 2):
        rider_groups.append([partial(mk_posm, j0)])
    for t0 in range(0, TO_T, 4):
        rider_groups.append([partial(mk_postr, t0)])
    for t0 in range(0, TO_T, 4):
        rider_groups.append([partial(mk_xdc, t0)])
    for t0 in range(0, TO_T, 4):
        rider_groups.append([partial(mk_b2, t0)])

    def prep_a(h):
        kpar = h % 3

        ktr_waits = []
        for kt in range(TF_T):
            ktr_waits.append(ring_next(ringX, f"kev_{h}_{kt}"))

        @mk("tensor")
        def _(e, h=h, ktr_waits=ktr_waits):
            for kt in range(TF_T):
                par, wl = ktr_waits[kt]
                if wl:
                    w(e, wl)
                w(e, f"xnt_{kt}")
                inc(e.transpose(psX[par][0:HD, 0:LANE], xnt_head(kt, h), ident[:]),
                    "t", f"ktr_{h}_{kt}")

        @mk("vector")
        def _(e, h=h, kpar=kpar, ktr_waits=ktr_waits):
            for kt in range(TF_T):
                w(e, f"ktr_{h}_{kt}")
                if kt == 0 and h >= 3:
                    w(e, f"sqq_{h - 3}_{QCH - 1}")
                    w(e, f"S_{h - 3}_{NPAIR - 1}_{TF_T - 1}")
                par = ktr_waits[kt][0]
                inc(e.tensor_copy(kbl[kpar][0:HD, kt * 128:(kt + 1) * 128],
                                  psX[par][0:HD, 0:LANE]), "v", f"kev_{h}_{kt}")
            e.drain()
            inc(e.tensor_scalar(qbl[kpar][0:HD, :], kbl[kpar][0:HD, 0:T_own],
                                cT[:, h:h + 1], None, OP.mult), "v", f"qbd_{h}")

    def prep_b(h):
        kpar = h % 3
        for j in range(KCH):
            @mk("scalar")
            def _(e, h=h, j=j, kpar=kpar):
                w(e, f"kev_{h}_{min(4 * j + 3, TF_T - 1)}")
                if j >= 2:
                    w(e, f"nkm_{h}_{j - 2}")
                elif h >= 1:
                    w(e, f"nqm_{h - 1}_{QCH - 1}")
                inc(e.activation(sql[j % 2][:], kbl[kpar][0:HD, j * 512:(j + 1) * 512],
                                 AF.Square), "s", f"sqk_{h}_{j}")

            spar, swl = ring_next(ringS, f"nkev_{h}_{j}")

            @mk("tensor")
            def _(e, h=h, j=j, spar=spar, swl=swl):
                w(e, f"sqk_{h}_{j}")
                if swl:
                    w(e, swl)
                inc(e.matmul(psS[spar][0:1, 0:512], ones64[:], sql[j % 2][:],
                             start=True, stop=True), "t", f"nkm_{h}_{j}")

            @mk("vector")
            def _(e, h=h, j=j, spar=spar):
                w(e, f"nkm_{h}_{j}")
                if j == 0 and h >= 1:
                    w(e, f"sqk2_{h - 1}")
                e.drain()
                e.tensor_reduce(asml[0:1, 1:2], psS[spar][0:1, 0:512], AX.X, OP.max)
                e.drain()
                if j == 0:
                    i1 = e.tensor_copy(asml[0:1, 0:1], asml[0:1, 1:2])
                else:
                    i1 = e.tensor_tensor(asml[0:1, 0:1], asml[0:1, 1:2],
                                         asml[0:1, 0:1], OP.max)
                inc(i1, "v", f"nkev_{h}_{j}")
                e.drain()

        for j in range(QCH):
            @mk("scalar")
            def _(e, h=h, j=j, kpar=kpar):
                w(e, f"qbd_{h}")
                if j >= 2:
                    w(e, f"nqm_{h}_{j - 2}")
                else:
                    w(e, f"nkm_{h}_{KCH - 1}")
                inc(e.activation(sql[j % 2][:], qbl[kpar][0:HD, j * 512:(j + 1) * 512],
                                 AF.Square), "s", f"sqq_{h}_{j}")

            spar, swl = ring_next(ringS, f"nqev_{h}_{j}")

            @mk("tensor")
            def _(e, h=h, j=j, spar=spar, swl=swl):
                w(e, f"sqq_{h}_{j}")
                if swl:
                    w(e, swl)
                inc(e.matmul(psS[spar][0:1, 0:512], ones64[:], sql[j % 2][:],
                             start=True, stop=True), "t", f"nqm_{h}_{j}")

            @mk("vector")
            def _(e, h=h, j=j, spar=spar):
                w(e, f"nqm_{h}_{j}")
                if j == 0 and h >= 1:
                    w(e, f"qm_{h - 1}")
                    w(e, f"sqq2_{h - 1}")
                e.drain()
                inc(e.tensor_copy(arow[0:1, j * 512:(j + 1) * 512],
                                  psS[spar][0:1, 0:512]), "v", f"nqev_{h}_{j}")

        @mk("scalar")
        def _(e, h=h):
            w(e, f"nkev_{h}_{KCH - 1}")
            if h >= 1:
                w(e, f"negm_{h - 1}")
            inc(e.activation(asml[0:1, 2:3], asml[0:1, 0:1], AF.Sqrt), "s", f"sqk2_{h}")
            w(e, f"nqev_{h}_{QCH - 1}")
            inc(e.activation(arow[0:1, 0:T_own], arow[0:1, 0:T_own], AF.Sqrt),
                "s", f"sqq2_{h}")

        @mk("vector")
        def _(e, h=h):
            w(e, f"sqq2_{h}")
            inc(e.tensor_scalar(arow[0:1, 0:T_own], arow[0:1, 0:T_own],
                                asml[0:1, 2:3], consts[0:1, 2:3], OP.mult, OP.mult),
                "v", f"negm_{h}")

        @mk("sync")
        def _(e, h=h, kpar=kpar):
            w(e, f"negm_{h}")
            if h >= 3:
                w(e, f"S_{h - 3}_{NPAIR - 1}_{TF_T - 1}")
            inc(e.dma_start(out=qbl[kpar][HD:WA, :], in_=arow[0:1, 0:T_own]),
                "qmh", f"qm_{h}", 16)

        @mk("gpsimd")
        def _(e, h=h, kpar=kpar):
            if h >= 3:
                w(e, f"S_{h - 3}_{NPAIR - 1}_{TF_T - 1}")
            elif h < 2:
                w(e, f"xnt_{TF_T - 1}")
                w(e, f"trdone_{TO_T - 1}")
            inc(e.memset(kbl[kpar][HD:WA, :], 1.0), "g", f"kone_{h}")

    def emit_ofin(hp, pp, otrE):
        q0p = pp * PB

        @mk("vector")
        def _(e, hp=hp, pp=pp, q0p=q0p, otrE=otrE):
            for j in range(JT):
                par = otrE[j][0]
                w(e, f"otr_{hp}_{pp}_{j}")
                tt = (q0p + j * 128) // 128
                e.drain()
                e.reciprocal(lw_recip[:], psX[par][:, HD:WA])
                e.drain()
                sl = acc[:, tt * D + hp * HD: tt * D + (hp + 1) * HD]
                inc(e.scalar_tensor_tensor(sl, psX[par][:, 0:HD], lw_recip[:],
                                           sl, OP.mult, OP.add),
                    "v", f"ofin_{hp}_{pp}_{j}")

    def emit_out_tail(hp, pp):
        oparp = (hp * NPAIR + pp) % 2
        otrE = [ring_next(ringX, f"ofin_{hp}_{pp}_{j}") for j in range(JT)]

        @mk("tensor")
        def _(e, hp=hp, pp=pp, oparp=oparp, otrE=otrE):
            for j in range(JT):
                half = j // 4 if NHALF > 1 else 0
                par, wl = otrE[j]
                w(e, f"oev_{hp}_{pp}_{half}")
                if wl:
                    w(e, wl)
                inc(e.transpose(psX[par][:, 0:WA],
                                obl[oparp][:, j * 128:(j + 1) * 128],
                                ident[0:WA, 0:WA]), "t", f"otr_{hp}_{pp}_{j}")

        emit_ofin(hp, pp, otrE)

    def main(h, p, riders=None, prev=None):
        kpar = h % 3
        q0 = p * PB
        pl = h * NPAIR + p
        opar = pl % 2

        s_entries = []
        for kt in range(TF_T):
            s_entries.append(ring_next(ringS, f"exp_{h}_{p}_{kt}"))

        KOFF = min(4, TF_T - JT)
        otrE = None
        if prev is not None:
            otrE = [ring_next(ringX, f"ofin_{prev[0]}_{prev[1]}_{j}")
                    for j in range(JT)]

        @mk("tensor")
        def _(e, h=h, p=p, q0=q0, kpar=kpar, pl=pl, s_entries=s_entries,
              prev=prev, otrE=otrE, KOFF=KOFF):
            for kt in range(TF_T):
                spar, swl = s_entries[kt]
                if swl:
                    w(e, swl)
                if kt == 0:
                    w(e, f"qm_{h}")
                    w(e, f"kone_{h}")
                    w(e, f"qbd_{h}")
                    w(e, f"kev_{h}_{TF_T - 1}")
                i1 = None
                for half in range(NHALF):
                    i1 = e.matmul(psS[spar][:, half * 512:(half + 1) * 512],
                                  kbl[kpar][:, kt * 128:(kt + 1) * 128],
                                  qbl[kpar][:, q0 + half * 512:q0 + (half + 1) * 512],
                                  start=True, stop=True)
                inc(i1, "t", f"S_{h}_{p}_{kt}")
                if kt >= 2:
                    ktp = kt - 2
                    w(e, f"exp_{h}_{p}_{ktp}")
                    for half in range(NHALF):
                        if ktp == 0 and pl >= 1:
                            pp = pl - 1
                            w(e, f"oev_{pp // NPAIR}_{pp % NPAIR}_{half}")
                        elif ktp == 0 and pl == 0 and half == 0:
                            w(e, f"lge_{TO_T - 1}")
                        i2 = e.matmul(psO[half][:, :], xnt_vaug(ktp, h),
                                      pbl[ktp % 4][:, half * 512:(half + 1) * 512],
                                      start=(ktp == 0), stop=False)
                        inc(i2, "t", f"pv_{h}_{p}_{ktp}_{half}")
                if otrE is not None and KOFF <= kt < KOFF + JT:
                    j = kt - KOFF
                    hp, pp = prev
                    oparp = (hp * NPAIR + pp) % 2
                    half = j // 4 if NHALF > 1 else 0
                    par, wl = otrE[j]
                    w(e, f"oev_{hp}_{pp}_{half}")
                    if wl:
                        w(e, wl)
                    inc(e.transpose(psX[par][:, 0:WA],
                                    obl[oparp][:, j * 128:(j + 1) * 128],
                                    ident[0:WA, 0:WA]), "t", f"otr_{hp}_{pp}_{j}")
            for ktp in (TF_T - 2, TF_T - 1):
                w(e, f"exp_{h}_{p}_{ktp}")
                for half in range(NHALF):
                    i2 = e.matmul(psO[half][:, :], xnt_vaug(ktp, h),
                                  pbl[ktp % 4][:, half * 512:(half + 1) * 512],
                                  start=False, stop=(ktp == TF_T - 1))
                    inc(i2, "t", f"pv_{h}_{p}_{ktp}_{half}")

        if otrE is not None:
            emit_ofin(prev[0], prev[1], otrE)

        @mk("scalar")
        def _(e, h=h, p=p, pl=pl, opar=opar, s_entries=s_entries):
            for kt in range(TF_T):
                spar = s_entries[kt][0]
                w(e, f"S_{h}_{p}_{kt}")
                if kt >= 4:
                    w(e, f"pv_{h}_{p}_{kt - 4}_{NHALF - 1}")
                elif p >= 1:
                    w(e, f"pv_{h}_{p - 1}_{TF_T - 4 + kt}_{NHALF - 1}")
                elif h >= 1:
                    w(e, f"pv_{h - 1}_{NPAIR - 1}_{TF_T - 4 + kt}_{NHALF - 1}")
                inc(e.activation(pbl[kt % 4][:], psS[spar][:], AF.Exp),
                    "s", f"exp_{h}_{p}_{kt}")
            for half in range(NHALF):
                w(e, f"pv_{h}_{p}_{TF_T - 1}_{half}")
                if pl >= 2:
                    pp = pl - 2
                    w(e, f"otr_{pp // NPAIR}_{pp % NPAIR}_{JT - 1}")
                inc(e.activation(obl[opar][:, half * 512:(half + 1) * 512],
                                 psO[half][:, :], AF.Copy), "s", f"oev_{h}_{p}_{half}")

        if riders:
            for th in riders:
                th()

    prep_a(0)
    prep_b(0)
    if H > 1:
        prep_a(1)
        prep_b(1)
    def mk_precast(idx):
        eidx, q = idx // NQ, idx % NQ

        @mk("gpsimd")
        def _(e, eidx=eidx, q=q):
            inc(e.dma_start(out=w1b_h[eidx, q], in_=w1_h[eidx, q]), "wc", None, 16)
            inc(e.dma_start(out=w2b_h[eidx, q], in_=w2_h[eidx, q]), "wc", None, 16)

    rq = list(rider_groups)
    prevp = None
    pcn = 0
    for h in range(H):
        for p in range(NPAIR):
            main(h, p, rq.pop(0) if rq else None, prev=prevp)
            prevp = (h, p)
            if pcn < NPH:
                mk_precast(pcn)
                pcn += 1
            if p == 0 and 2 <= h + 1 < H and h >= 1:
                prep_b(h + 1)
        if h + 2 < H:
            prep_a(h + 2)
    emit_out_tail(H - 1, NPAIR - 1)
    while rq:
        for th in rq.pop(0):
            th()
    while pcn < NPH:
        mk_precast(pcn)
        pcn += 1

    for cm in reversed(p3_cms):
        cm.__exit__(None, None, None)
    for cm in reversed(p3_pcms):
        cm.__exit__(None, None, None)
    for cm in reversed(p2keep_cms):
        cm.__exit__(None, None, None)
    xnt_cm.__exit__(None, None, None)

    assert CAP <= 512
    wq1_0, _ = sb("wq1_0", [LANE, DC * FQ], BF16)
    wq1_1, _ = sb("wq1_1", [LANE, DC * FQ], BF16)
    wq1_2, _ = sb("wq1_2", [LANE, DC * FQ], BF16)
    wq1 = [wq1_0, wq1_1, wq1_2]
    wq2_0, _ = sb("wq2_0", [LANE, FT_Q * D], BF16)
    wq2_1, _ = sb("wq2_1", [LANE, FT_Q * D], BF16)
    wq2_2, _ = sb("wq2_2", [LANE, FT_Q * D], BF16)
    wq2 = [wq2_0, wq2_1, wq2_2]
    hbuf, _ = sb("hbuf", [LANE, FT_Q * CAP], BF16)
    xgt_0, _ = sb("xgt_0", [LANE, DC * CAP], BF16)
    xgt_1, _ = sb("xgt_1", [LANE, DC * CAP], BF16)
    xgt = [xgt_0, xgt_1]
    yacc, _ = sb("yacc", [LANE, CT * D], F32)
    xstg_0, _ = sb("xstg_0", [LANE, D], BF16)
    xstg_1, _ = sb("xstg_1", [LANE, D], BF16)
    xstg = [xstg_0, xstg_1]

    psH0, _ = pst("psH0", [LANE, 512], F32)
    psH1, _ = pst("psH1", [LANE, 512], F32)
    psH = [psH0, psH1]
    psY0, _ = pst("psY0", [LANE, 512], F32)
    psY1, _ = pst("psY1", [LANE, 512], F32)
    psY = [psY0, psY1]
    psG0, _ = pst("psG0", [LANE, LANE], BF16)
    psG1, _ = pst("psG1", [LANE, LANE], BF16)
    psG2, _ = pst("psG2", [LANE, LANE], BF16)
    psG3, _ = pst("psG3", [LANE, LANE], BF16)
    psG = [psG0, psG1, psG2, psG3]

    def mk_wload(idx):
        eidx, q = idx // NQ, idx % NQ

        @mk("gpsimd")
        def _(e, idx=idx, eidx=eidx, q=q):
            par = idx % 3
            if idx <= 2:
                w(e, "sct_done0")
                w(e, "sct_done1")
                w(e, f"exp_{H - 1}_{NPAIR - 1}_{TF_T - 1}")
                w(e, f"oev_{H - 1}_{NPAIR - 1}_{NHALF - 1}")
                w(e, f"ofin_{H - 1}_{NPAIR - 1}_{JT - 1}")
                w(e, f"otr_{H - 1}_{NPAIR - 1}_{JT - 1}")
                w(e, f"b2e_{TO_T - 1}_{1}")
                e.wait_ge(SEMS["wc"], NPH * 32)
            if idx >= 3:
                w(e, f"ffn_w_done_{idx - 3}")
            inc(e.dma_start(out=wq1[par][:], in_=w1b_h[eidx, q]), f"ldw{par}", None, 16)
            inc(e.dma_start(out=wq2[par][:], in_=w2b_h[eidx, q]), f"ldw{par}", f"w12l_{idx}", 16)

    mk_wload(0)
    if NPH > 1:
        mk_wload(1)
    if NPH > 2:
        mk_wload(2)

    def ffn_stage(eidx):
        epar = eidx % 2
        for st in range(CT):
            @mk("sync")
            def _(e, eidx=eidx, st=st, epar=epar):
                g = eidx * CT + st
                spar = g % 2
                if eidx == 0 and st == 0:
                    w(e, "sct_done0")
                    w(e, "sct_done1")
                    w(e, f"ofin_{H - 1}_{NPAIR - 1}_{JT - 1}")
                    w(e, f"exp_{H - 1}_{NPAIR - 1}_{TF_T - 1}")
                    w(e, f"oev_{H - 1}_{NPAIR - 1}_{NHALF - 1}")
                    w(e, f"otr_{H - 1}_{NPAIR - 1}_{JT - 1}")
                    w(e, f"b2e_{TO_T - 1}_{1}")
                if g >= 2:
                    w(e, f"xgtr_{(g - 2) // CT}_{(g - 2) % CT}_{DC - 1}")
                r0 = eidx * CAP + st * 128
                inc(e.dma_start(out=xstg[spar][:], in_=xg_h[r0:r0 + 128, :]),
                    f"xg{spar}", f"xgl_{eidx}_{st}", 16)

            @mk("tensor")
            def _(e, eidx=eidx, st=st, epar=epar):
                g = eidx * CT + st
                spar = g % 2
                w(e, f"xgl_{eidx}_{st}")
                for c in range(DC):
                    u = g * DC + c
                    par = u % 4
                    if u >= 4:
                        w(e, f"xgte_{((u - 4) // DC) // CT}_{((u - 4) // DC) % CT}_{(u - 4) % DC}")
                    else:
                        w(e, f"ofin_{H - 1}_{NPAIR - 1}_{JT - 1}")
                    inc(e.transpose(psG[par][:], xstg[spar][:, c * 128:(c + 1) * 128],
                                    ident[:]), "t", f"xgtr_{eidx}_{st}_{c}")

            @mk("vector")
            def _(e, eidx=eidx, st=st, epar=epar):
                for c in range(DC):
                    u = (eidx * CT + st) * DC + c
                    w(e, f"xgtr_{eidx}_{st}_{c}")
                    if st == 0 and c == 0 and eidx >= 2:
                        w(e, f"h_{(eidx - 2) * NQ + NQ - 1}_{FT_Q - 1}")
                    inc(e.tensor_copy(
                        xgt[epar][:, c * CAP + st * 128: c * CAP + (st + 1) * 128],
                        psG[u % 4][:]), "v", f"xgte_{eidx}_{st}_{c}")

    ffn_stage(0)
    for eidx in range(E):
        epar = eidx % 2
        for q in range(NQ):
            idx = eidx * NQ + q
            par = idx % 3
            if idx + 3 < NPH:
                mk_wload(idx + 3)
            if q == 0 and eidx + 1 < E:
                ffn_stage(eidx + 1)

            for ft in range(FT_Q):
                @mk("tensor")
                def _(e, idx=idx, par=par, ft=ft, eidx=eidx, epar=epar):
                    hp = ft % 2
                    w(e, f"w12l_{idx}")
                    if ft >= 2:
                        w(e, f"gel_{idx}_{ft - 2}")
                    elif idx >= 1:
                        w(e, f"gel_{idx - 1}_{FT_Q - 2 + ft}")
                    if ft == 0:
                        w(e, f"xgte_{eidx}_{CT - 1}_{DC - 1}")
                    i1 = None
                    for c in range(DC):
                        i1 = e.matmul(psH[hp][:, 0:CAP],
                                      wq1[par][:, c * FQ + ft * 128:c * FQ + (ft + 1) * 128],
                                      xgt[epar][:, c * CAP:(c + 1) * CAP],
                                      start=(c == 0), stop=(c == DC - 1))
                    inc(i1, "t", f"h_{idx}_{ft}")

                @mk("scalar")
                def _(e, idx=idx, eidx=eidx, q=q, ft=ft):
                    hp = ft % 2
                    w(e, f"h_{idx}_{ft}")
                    if ft == 0 and idx >= 1:
                        w(e, f"y_{idx - 1}_{CT - 1}_1")
                    bcol = eidx * 32 + q * FT_Q + ft
                    inc(e.activation(hbuf[:, ft * CAP:(ft + 1) * CAP],
                                     psH[hp][:, 0:CAP], AF.Gelu,
                                     bias=b1_s[:, bcol:bcol + 1], scale=1.0),
                        "s", f"gel_{idx}_{ft}")

            for st in range(CT):
                for dch in range(2):
                    @mk("tensor")
                    def _(e, idx=idx, par=par, st=st, dch=dch):
                        u = st * 2 + dch
                        yp = u % 2
                        w(e, f"w12l_{idx}")
                        w(e, f"gel_{idx}_{FT_Q - 1}")
                        if u >= 2:
                            w(e, f"yev_{idx}_{(u - 2) // 2}_{(u - 2) % 2}")
                        elif idx >= 1:
                            w(e, f"yev_{idx - 1}_{CT - 1}_{u}")
                        i1 = None
                        for c in range(FT_Q):
                            i1 = e.matmul(psY[yp][:, :],
                                          hbuf[:, c * CAP + st * 128:c * CAP + (st + 1) * 128],
                                          wq2[par][:, c * D + dch * 512:c * D + (dch + 1) * 512],
                                          start=(c == 0), stop=(c == FT_Q - 1))
                        inc(i1, "t", f"y_{idx}_{st}_{dch}")
                        if st == CT - 1 and dch == 1:
                            M[f"ffn_w_done_{idx}"] = M[f"y_{idx}_{st}_{dch}"]

                    @mk("vector")
                    def _(e, idx=idx, eidx=eidx, q=q, st=st, dch=dch):
                        u = st * 2 + dch
                        yp = u % 2
                        w(e, f"y_{idx}_{st}_{dch}")
                        if q == 0 and u == 0 and eidx >= 1:
                            w(e, f"ysd_{eidx - 1}")
                        sl = yacc[:, st * D + dch * 512: st * D + (dch + 1) * 512]
                        if q == 0:
                            inc(e.tensor_copy(sl, psY[yp][:, :]), "v",
                                f"yev_{idx}_{st}_{dch}")
                        else:
                            inc(e.tensor_tensor(sl, psY[yp][:, :], sl, OP.add), "v",
                                f"yev_{idx}_{st}_{dch}")

        @mk("gpsimd")
        def _(e, eidx=eidx):
            idx = eidx * NQ + NQ - 1
            w(e, f"yev_{idx}_{CT - 1}_{1}")
            r0 = eidx * CAP
            lbl = "ys_all" if eidx == E - 1 else f"ysd_{eidx}"
            inc(e.dma_start(out=yscr_h[r0:r0 + CAP, :].rearrange("(t p) d -> p t d", p=LANE),
                            in_=yacc[:].rearrange("p (t d) -> p t d", d=D)),
                "ys", lbl, 16)
            if eidx == E - 1:
                M[f"ysd_{eidx}"] = M["ys_all"]

    for t in range(TO_T):
        @mk("gpsimd")
        def _(e, t=t):
            w(e, "ys_all")
            if t >= 2:
                w(e, f"gadd_{t - 2}")
            inc(e.indirect_dma_start(out=xstg[t % 2][:], out_offset=None,
                                     in_=yscr_h, in_offset=IOA(pos_i[:, t:t + 1], 0)),
                f"gt{t % 2}", f"gth_{t}", 16)

        @mk("vector")
        def _(e, t=t):
            w(e, f"gth_{t}")
            e.drain()
            e.tensor_reduce(lw_recip[:], gates[:, t * E:(t + 1) * E], AX.X, OP.add)
            e.drain()
            sl = acc[:, t * D:(t + 1) * D]
            inc(e.scalar_tensor_tensor(sl, xstg[t % 2][:], lw_recip[:], sl,
                                       OP.mult, OP.add), "v", f"gadd_{t}")

    @mk("sync")
    def _(e):
        w(e, f"gadd_{TO_T - 1}")
        inc(e.dma_start(out=out_h.rearrange("(t p) d -> p t d", p=LANE),
                        in_=acc[:].rearrange("p (t d) -> p t d", d=D)),
            "o", "out_done", 16)
        e.wait_ge(SEMS["o"], 16)

    C.clear()
    mock = _MockEng()
    for eng, fn in chunks:
        fn(mock)
    C.clear()
    strict[0] = True
    blk_cm = nc.Block()
    block = blk_cm.__enter__()
    for engname in ["sync", "gpsimd", "vector", "scalar", "tensor"]:
        def section(handle, engname=engname):
            for eng, fn in chunks:
                if eng == engname:
                    fn(handle)
        getattr(block, engname)(section)
    blk_cm.__exit__(None, None, None)
    return nc


def _host_layouts(W1, b1, W2, b2, Wr, br, rot, ent):
    w1 = np.ascontiguousarray(
        W1.reshape(E, DC, LANE, NQ, FQ).transpose(0, 3, 2, 1, 4)
    ).reshape(E, NQ, LANE, DC * FQ)
    w2 = np.ascontiguousarray(
        W2.reshape(E, NQ, FT_Q, LANE, D).transpose(0, 1, 3, 2, 4)
    ).reshape(E, NQ, LANE, FT_Q * D)
    b1l = np.ascontiguousarray(b1.reshape(E, 32, LANE).transpose(2, 0, 1)).reshape(LANE, E * 32)
    wr = np.ascontiguousarray(Wr.reshape(DC, LANE, E).transpose(1, 0, 2)).reshape(LANE, DC * E)
    brl = np.ascontiguousarray(br.reshape(E, 1))
    rotT = np.ascontiguousarray(rot.T)
    entT = np.ascontiguousarray(ent.T)
    return dict(w1=w1, w2=w2, b1=b1l, b2=np.ascontiguousarray(b2), wr=wr, br=brl,
                rotT=rotT, entT=entT)


def kernel(L0_attn_inputs, L0_attn_rotation_params, L0_attn_entangle_params,
           L0_ffn_W1, L0_ffn_b1, L0_ffn_W2, L0_ffn_b2, L0_ffn_Wr, L0_ffn_br):
    x = np.ascontiguousarray(np.asarray(L0_attn_inputs, dtype=np.float32))
    B, T, _ = x.shape
    T_own = T // 2
    params = _host_layouts(np.asarray(L0_ffn_W1, np.float32),
                           np.asarray(L0_ffn_b1, np.float32),
                           np.asarray(L0_ffn_W2, np.float32),
                           np.asarray(L0_ffn_b2, np.float32),
                           np.asarray(L0_ffn_Wr, np.float32),
                           np.asarray(L0_ffn_br, np.float32),
                           np.asarray(L0_attn_rotation_params, np.float32),
                           np.asarray(L0_attn_entangle_params, np.float32))
    nc = build_program(T, T_own)
    cap = T_own // 4
    params["xg"] = np.zeros((E * cap, D), dtype=ml_dtypes.bfloat16)
    params["bases"] = (np.arange(E, dtype=np.float32) * cap - 1.0).reshape(E, 1)
    in_maps = []
    for core in range(8):
        b, half = core // 2, core % 2
        if half == 0:
            xr = x[b]
        else:
            xr = np.ascontiguousarray(np.concatenate([x[b, T_own:], x[b, :T_own]], axis=0))
        m = dict(params)
        m["xf"] = xr
        in_maps.append(m)
    res = run_bass_kernel_spmd(nc, in_maps, core_ids=list(range(8)))
    out = np.empty((B, T, D), np.float32)
    for core in range(8):
        b, half = core // 2, core % 2
        sl = slice(0, T_own) if half == 0 else slice(T_own, T)
        out[b, sl] = res.results[core]["out"]
    return out
